# version 19
# baseline (speedup 1.0000x reference)
import os
import numpy as np
from contextlib import ExitStack
import concourse.bass as bass
import concourse.mybir as mybir
from concourse.bass_utils import run_bass_kernel_spmd

F32 = mybir.dt.float32
BF16 = mybir.dt.bfloat16
FP8 = mybir.dt.float8e4
Alu = mybir.AluOpType
Act = mybir.ActivationFunctionType
AxX = mybir.AxisListType.X

P = 128
DM = 1024
DF = 2816
KC = DM // P
NCH = DF // P
MACRO = 512
SUB = 128
SPM = MACRO // SUB
LN_EPS = 1e-5
ATT_EPS = 1e-6
QCLAMP = 1e-12

SLABS = [(i * 512, min(512, DF - i * 512)) for i in range((DF + 511) // 512)]
NSL = len(SLABS)
PHASES = int(os.environ.get("KPH", "4"))
L1 = int(os.environ.get("KL1", "9"))


def build_program(seq_len=4096, n_seq=2):
    T = seq_len * n_seq
    NMAC = T // MACRO
    NMAC_S = seq_len // MACRO
    NSUB = T // SUB

    nc = bass.Bass("TRN2", target_bir_lowering=False)

    x_h = nc.dram_tensor("x", [T, DM], F32, kind="ExternalInput").ap()
    wqT_h = nc.dram_tensor("wqT", [DM, DM], F32, kind="ExternalInput").ap()
    wkT_h = nc.dram_tensor("wkT", [DM, DM], F32, kind="ExternalInput").ap()
    wvT_h = nc.dram_tensor("wvT", [DM, DM], F32, kind="ExternalInput").ap()
    woT_h = nc.dram_tensor("woT", [DM, DM], F32, kind="ExternalInput").ap()
    wg_h = nc.dram_tensor("wg", [DF, DM], F32, kind="ExternalInput").ap()
    wu_h = nc.dram_tensor("wu", [DF, DM], F32, kind="ExternalInput").ap()
    wd_h = nc.dram_tensor("wd", [DM, DF], F32, kind="ExternalInput").ap()
    y_h = nc.dram_tensor("y", [T, DM], F32, kind="ExternalOutput").ap()
    q_s = nc.dram_tensor("q_s", [NMAC, P, KC * MACRO], BF16).ap()
    x2_s = nc.dram_tensor("x2_s", [T, DM], F32).ap()
    xq_s = nc.dram_tensor("xq_s", [NSUB, P, KC * SUB], BF16).ap()
    qwg_s = nc.dram_tensor("qwg_s", [KC, P, DF], BF16).ap()
    qwu_s = nc.dram_tensor("qwu_s", [KC, P, DF], BF16).ap()
    qwd_s = nc.dram_tensor("qwd_s", [NCH, P, DM], BF16).ap()
    CONSTS = [4.0, 7.0, 6.0, -1.0, -9.0, float(LN_EPS), -0.5, 1.0, 0.0]
    cst_h = nc.dram_tensor("cst", [P, len(CONSTS)], F32, kind="ExternalInput").ap()

    es = ExitStack()
    with es:
        E = es.enter_context
        ident = E(nc.sbuf_tensor("ident", [P, P], BF16))
        ones1 = E(nc.sbuf_tensor("ones1", [P, 1], BF16))

        cst = E(nc.sbuf_tensor("cst_sb", [P, len(CONSTS)], F32))
        for _ci, _cv in enumerate(CONSTS):
            nc.const_aps.register(F32, _cv, cst[:, _ci:_ci + 1])
        wo_es = ExitStack()
        kvb = wo_es.enter_context(nc.sbuf_tensor("kvb", [P, n_seq, KC, P], BF16))
        KSb = wo_es.enter_context(nc.sbuf_tensor("KSb", [P, n_seq, KC, 16], BF16))
        attn3_es = ExitStack()
        wqT = attn3_es.enter_context(nc.sbuf_tensor("wqT_sb", [P, KC, DM], BF16))
        wkT = attn3_es.enter_context(nc.sbuf_tensor("wkT_sb", [P, KC, DM], BF16))
        wvT = attn3_es.enter_context(nc.sbuf_tensor("wvT_sb", [P, KC, DM], BF16))

        def sem(name):
            return E(nc.semaphore(name))

        ACL = [("a", wi, c) for wi in range(3) for c in range(KC)]
        FCL = [("g", i, 0, DM) for i in range(NCH)] \
            + [("u", i, 0, DM) for i in range(NCH)] \
            + [("d", i, pc, (768 if pc == 2 else 1024))
               for i in range(KC) for pc in range(3)]
        NA, NF = len(ACL), len(FCL)

        s_id = sem("s_id")
        s_cst = sem("s_cst")
        s_wld = [sem("s_wld0"), sem("s_wld1")]
        s_wcast = sem("s_wcast")
        s_q_ab = sem("s_q_ab")
        s_q_u = sem("s_q_u")
        s_q_s = sem("s_q_s")
        s_q_sg = sem("s_q_sg")
        s_q_q = sem("s_q_q")
        s_p0tr = sem("s_p0tr")
        s_p0ev = sem("s_p0ev")
        s_p0sp = [sem("s_p0sp0"), sem("s_p0sp1")]
        s_xld = [sem("s_xld0"), sem("s_xld1")]
        s_vst = sem("s_vst")
        s_ssq = sem("s_ssq")
        s_srs = sem("s_srs")
        s_lnd = sem("s_lnd")
        s_tr = sem("s_tr")
        s_trev = sem("s_trev")
        s_qmm = sem("s_qmm")
        s_qe = sem("s_qe")
        s_qev = sem("s_qev")
        s_qsp = sem("s_qsp")
        s_kmm = sem("s_kmm")
        s_ke = sem("s_ke")
        s_kev = sem("s_kev")
        s_vmm = sem("s_vmm")
        s_vev = sem("s_vev")
        s_kvr = sem("s_kvr")
        s_kvev = sem("s_kvev")

        def f_of(M):
            if M < 0:
                return []
            lo = NF * M // NMAC
            hi = NF * (M + 1) // NMAC
            return list(range(lo, hi))

        def f_sub(M, j):
            return [m for e, m in enumerate(f_of(M - 1)) if min(e, SPM - 1) == j]

        def fsrc(which, i, pc, F):
            src = {"g": wg_h, "u": wu_h, "d": wd_h}[which]
            return src[i * P:(i + 1) * P, pc * 1024:pc * 1024 + F]

        with ExitStack() as _pes:
            wst = _pes.enter_context(nc.sbuf_tensor("wst", [P, 2, DM], F32))
            p0u = _pes.enter_context(nc.sbuf_tensor("p0u", [P, 2, DM], F32))
            p0ab = _pes.enter_context(nc.sbuf_tensor("p0ab", [P, 2, DM], F32))
            p0v1 = _pes.enter_context(nc.sbuf_tensor("p0v1", [P, 2, DM], FP8))
            p0v2 = _pes.enter_context(nc.sbuf_tensor("p0v2", [P, 2, DM], FP8))
            p0r = _pes.enter_context(nc.sbuf_tensor("p0r", [P, 2, DM], BF16))
            p0B = _pes.enter_context(nc.sbuf_tensor("p0B", [P, 2, DM], BF16))
            p0c2 = _pes.enter_context(nc.sbuf_tensor("p0c2", [P, 2, DM], BF16))
            p0g = _pes.enter_context(nc.sbuf_tensor("p0g", [P, 2, DM], BF16))
            p0sg = _pes.enter_context(nc.sbuf_tensor("p0sg", [P, 2, DM], BF16))
            p0q = _pes.enter_context(nc.sbuf_tensor("p0q", [P, 6, DM], BF16))
            p0bm = _pes.enter_context(nc.sbuf_tensor("p0bm", [P, 2, DM // 16], F32))
            p0rr = _pes.enter_context(nc.sbuf_tensor("p0rr", [P, 2, DM // 16], F32))
            p0sb = _pes.enter_context(nc.sbuf_tensor("p0sb", [P, 2, DM // 16], BF16))
            p0st2 = _pes.enter_context(nc.sbuf_tensor("p0st2", [P, 2, DM], BF16))
            xbuf = _pes.enter_context(nc.sbuf_tensor("xbuf", [P, 2, SPM, DM], F32))
            stt1 = _pes.enter_context(nc.sbuf_tensor("stt1", [P, 24], F32))
            xn1 = _pes.enter_context(nc.sbuf_tensor("xn1", [P, 2, SPM, DM], BF16))
            xn1T = _pes.enter_context(nc.sbuf_tensor("xn1T", [P, KC, MACRO], BF16))
            qsb = _pes.enter_context(nc.sbuf_tensor("qsb", [P, KC, MACRO], BF16))
            fme = _pes.enter_context(nc.sbuf_tensor("fme", [P, 2, 512], BF16))
            fma = _pes.enter_context(nc.sbuf_tensor("fma", [P, 2, 512], BF16))
            ksb = _pes.enter_context(nc.sbuf_tensor("ksb", [P, DM], BF16))
            vsb = _pes.enter_context(nc.sbuf_tensor("vsb", [P, DM], BF16))
            kvp_ps = _pes.enter_context(nc.psum_tensor("kvp_ps", [P, KC, P], F32))
            ks_ps = _pes.enter_context(nc.psum_tensor("ks_ps", [P, KC], F32))
            q_ps0 = _pes.enter_context(nc.psum_tensor("q_ps0", [P, 512], F32))
            q_ps1 = _pes.enter_context(nc.psum_tensor("q_ps1", [P, 512], F32))
            k_ps = _pes.enter_context(nc.psum_tensor("k_ps", [P, DM], F32))
            tr_ps = _pes.enter_context(nc.psum_tensor("tr_ps", [P, KC * P], BF16))
            blk = _pes.enter_context(nc.Block())

            @blk.sync
            def _(sy):
                sy.dma_start(cst[:], cst_h[:]).then_inc(s_cst, 16)
                sy.dma_start(
                    xbuf[:, 0, :, :],
                    x_h[0:MACRO, :].rearrange("(j p) d -> p j d", p=P)) \
                    .then_inc(s_xld[0], 16)
                sy.dma_start(
                    xbuf[:, 1, :, :],
                    x_h[MACRO:2 * MACRO, :].rearrange("(j p) d -> p j d", p=P)) \
                    .then_inc(s_xld[1], 16)
                for li, (_, wi, c) in enumerate(ACL):
                    wh = [wqT_h, wkT_h, wvT_h][wi]
                    if li >= 2:
                        sy.wait_ge(s_wcast, li - 1)
                    sy.dma_start(wst[:, li % 2, :],
                                 wh[c * P:(c + 1) * P, :]) \
                        .then_inc(s_wld[li % 2], 16)
                for M in range(2, NMAC):
                    sy.wait_ge(s_lnd, M - 1)
                    sy.wait_ge(s_ssq, SPM * (M - 1))
                    sy.dma_start(
                        xbuf[:, M % 2, :, :],
                        x_h[M * MACRO:(M + 1) * MACRO, :]
                            .rearrange("(j p) d -> p j d", p=P)) \
                        .then_inc(s_xld[M % 2], 16)
                    for fi in f_of(M - 2):
                        which, i, pc, F = FCL[fi]
                        li = NA + fi
                        if fi >= 2:
                            sy.wait_ge(s_q_u, fi - 1)
                            sy.wait_ge(s_q_sg, fi - 1)
                        else:
                            sy.wait_ge(s_wcast, NA - 1 + fi)
                        sy.dma_start(wst[:, li % 2, 0:F], fsrc(which, i, pc, F)) \
                            .then_inc(s_wld[li % 2], 16)
                for fi in range(NF * (NMAC - 2) // NMAC, NF):
                    which, i, pc, F = FCL[fi]
                    li = NA + fi
                    sy.wait_ge(s_q_u, fi - 1)
                    sy.wait_ge(s_q_sg, fi - 1)
                    sy.dma_start(wst[:, li % 2, 0:F], fsrc(which, i, pc, F)) \
                        .then_inc(s_wld[li % 2], 16)

            @blk.gpsimd
            def _(g):
                g.wait_ge(s_id, 1)
                nc.gpsimd.affine_select(
                    out=ident[:], in_=ident[:], compare_op=Alu.not_equal,
                    fill=1.0, base=0, pattern=[[-1, P]], channel_multiplier=1,
                ).then_inc(s_id, 1)
                for M in range(NMAC):
                    g.wait_ge(s_qev, 8 * M + 8)
                    g.dma_start(q_s[M], qsb[:].rearrange("p c t -> p (c t)")) \
                        .then_inc(s_qsp, 16)
                    for fi in f_of(M - 1) if M >= 1 else []:
                        which, i, pc, F = FCL[fi]
                        npc = F // P
                        g.wait_ge(s_p0ev, fi + 1)
                        src = p0st2[:, fi % 2, 0:npc * P] \
                            .rearrange("p (c t) -> p c t", t=P)
                        if which == "g":
                            dst = bass.AP(qwg_s.tensor, i * P,
                                          [[DF, P], [P * DF, npc], [1, P]])
                        elif which == "u":
                            dst = bass.AP(qwu_s.tensor, i * P,
                                          [[DF, P], [P * DF, npc], [1, P]])
                        else:
                            dst = bass.AP(qwd_s.tensor,
                                          (8 * pc) * P * DM + i * P,
                                          [[DM, P], [P * DM, npc], [1, P]])
                        g.dma_start(dst, src).then_inc(s_p0sp[fi % 2], 16)
                for fi in f_of(NMAC - 1):
                    which, i, pc, F = FCL[fi]
                    npc = F // P
                    g.wait_ge(s_p0ev, fi + 1)
                    src = p0st2[:, fi % 2, 0:npc * P] \
                        .rearrange("p (c t) -> p c t", t=P)
                    if which == "g":
                        dst = bass.AP(qwg_s.tensor, i * P,
                                      [[DF, P], [P * DF, npc], [1, P]])
                    elif which == "u":
                        dst = bass.AP(qwu_s.tensor, i * P,
                                      [[DF, P], [P * DF, npc], [1, P]])
                    else:
                        dst = bass.AP(qwd_s.tensor, (8 * pc) * P * DM + i * P,
                                      [[DM, P], [P * DM, npc], [1, P]])
                    g.dma_start(dst, src).then_inc(s_p0sp[fi % 2], 16)
                g.wait_ge(s_qsp, 16 * NMAC)
                g.wait_ge(s_p0sp[0], 16 * ((NF + 1) // 2))
                g.wait_ge(s_p0sp[1], 16 * (NF // 2))

            def v_ffn_pair(v, fis):
                fa = fis[0]
                fb = fis[-1]
                sts, nbs = [], []
                for fi in fis:
                    which, i, pc, F = FCL[fi]
                    li = NA + fi
                    sts.append(wst[:, li % 2, 0:F])
                    nbs.append(F // 16)
                    v.wait_ge(s_wld[li % 2], 16 * (li // 2 + 1))
                if fa >= 1:
                    v.wait_ge(s_q_s, fa - 1 if fa >= 2 else fa)
                if fb >= 6:
                    v.wait_ge(s_p0tr, fb - 5)
                for fi, st, nb in zip(fis, sts, nbs):
                    nc.vector.tensor_reduce(
                        p0bm[:, fi % 2, 0:nb, None],
                        st.rearrange("p (b k) -> p b k", k=16),
                        AxX, Alu.max, apply_absolute_value=True)
                v.drain()
                for fi, nb in zip(fis, nbs):
                    nc.vector.reciprocal(p0rr[:, fi % 2, 0:nb],
                                         p0bm[:, fi % 2, 0:nb])
                    nc.vector.tensor_scalar(p0sb[:, fi % 2, 0:nb],
                                            p0bm[:, fi % 2, 0:nb],
                                            1.0 / 6.0, QCLAMP, Alu.mult, Alu.max)
                v.drain()
                v.wait_ge(s_q_ab, fb + 1)
                ins = None
                for fi, nb in zip(fis, nbs):
                    F = nb * 16
                    ins = nc.vector.tensor_tensor(
                        p0u[:, fi % 2, 0:F].rearrange("p (b k) -> p b k", k=16),
                        p0ab[:, fi % 2, 0:F].rearrange("p (b k) -> p b k", k=16),
                        p0rr[:, fi % 2, 0:nb, None].broadcast_to([P, nb, 16]),
                        Alu.mult).then_inc(s_q_u, 1)
                v.drain()
                for fi, nb in zip(fis, nbs):
                    F = nb * 16
                    nc.vector.tensor_scalar(p0c2[:, fi % 2, 0:F],
                                            p0u[:, fi % 2, 0:F],
                                            5.0, 2.0, Alu.is_gt, Alu.mult)
                v.wait_ge(s_q_s, fb + 1)
                for fi, nb in zip(fis, nbs):
                    F = nb * 16
                    nc.vector.scalar_tensor_tensor(
                        p0g[:, fi % 2, 0:F], p0B[:, fi % 2, 0:F], 2.0,
                        p0r[:, fi % 2, 0:F], Alu.min, Alu.subtract)
                v.drain()
                for fi, nb in zip(fis, nbs):
                    F = nb * 16
                    nc.vector.scalar_tensor_tensor(
                        p0g[:, fi % 2, 0:F], p0c2[:, fi % 2, 0:F], 2.0,
                        p0g[:, fi % 2, 0:F], Alu.add, Alu.add)
                v.drain()
                v.wait_ge(s_q_sg, fb + 1)
                for fi, nb in zip(fis, nbs):
                    F = nb * 16
                    nc.vector.tensor_mul(p0g[:, fi % 2, 0:F],
                                         p0g[:, fi % 2, 0:F],
                                         p0sg[:, fi % 2, 0:F])
                v.drain()
                for fi, nb in zip(fis, nbs):
                    F = nb * 16
                    nc.vector.tensor_tensor(
                        p0q[:, fi % 6, 0:F].rearrange("p (b k) -> p b k", k=16),
                        p0g[:, fi % 2, 0:F].rearrange("p (b k) -> p b k", k=16),
                        p0sb[:, fi % 2, 0:nb, None].broadcast_to([P, nb, 16]),
                        Alu.mult).then_inc(s_q_q, 1)

            def v_ffn_evac(v, ci):
                F = FCL[ci][3]
                v.wait_ge(s_p0tr, ci + 1)
                if ci >= 2:
                    v.wait_ge(s_p0sp[ci % 2], 16 * (ci // 2))
                nc.vector.tensor_copy(
                    p0st2[:, ci % 2, 0:F],
                    tr_ps[:, 0:F]).then_inc(s_p0ev, 1)

            @blk.vector
            def _(v):
                nc.vector.memzero(ident[:]).then_inc(s_id, 1)
                nc.vector.memset(ones1[:], 1.0)
                for li, (_, wi, c) in enumerate(ACL):
                    wsb = [wqT, wkT, wvT][wi]
                    v.wait_ge(s_wld[li % 2], 16 * (li // 2 + 1))
                    nc.vector.tensor_copy(wsb[:, c, :], wst[:, li % 2, :]) \
                        .then_inc(s_wcast, 1)
                mcol = stt1[:, 0:SPM]
                sscol = stt1[:, 4:4 + SPM]
                vcol = stt1[:, 8:8 + SPM]
                m2col = stt1[:, 12:12 + SPM]
                rcol = stt1[:, 16:16 + SPM]
                for it in range(NMAC + 1):
                    if it < NMAC:
                        xb = xbuf[:, it % 2, :, :]
                        v.wait_ge(s_xld[it % 2], 16 * (it // 2 + 1))
                        if it >= 1:
                            v.drain()
                        nc.vector.tensor_reduce(mcol[:, :, None], xb, AxX,
                                                Alu.add)
                        v.drain()
                        nc.vector.tensor_scalar(mcol, mcol, 1.0 / DM, None,
                                                Alu.mult)
                        v.drain()
                        nc.vector.tensor_mul(m2col, mcol, mcol)
                        v.drain()
                        v.wait_ge(s_ssq, SPM * (it + 1))
                        nc.vector.scalar_tensor_tensor(
                            vcol, sscol, 1.0 / DM, m2col,
                            Alu.mult, Alu.subtract).then_inc(s_vst, 1)
                        v.wait_ge(s_srs, it + 1)
                        if it >= 2:
                            v.wait_ge(s_tr, 4 * it - 4)
                        for j in range(SPM):
                            ins = nc.vector.tensor_scalar(
                                xn1[:, it % 2, j, :], xb[:, j, :],
                                mcol[:, j:j + 1], rcol[:, j:j + 1],
                                Alu.subtract, Alu.mult)
                        ins.then_inc(s_lnd, 1)
                    if it == 0:
                        continue
                    M = it - 1
                    sq = M // NMAC_S
                    m_in_s = M % NMAC_S
                    if M >= 1:
                        v.wait_ge(s_qsp, 16 * M)
                    for ic in range(KC):
                        v.wait_ge(s_qe, 2 * (8 * M + ic + 1))
                        nc.vector.scalar_tensor_tensor(
                            qsb[:, ic, :], fma[:, ic % 2, :], 1.0,
                            fme[:, ic % 2, :], Alu.add, Alu.min) \
                            .then_inc(s_qev, 1)
                    for j in range(SPM):
                        jj = SPM * M + j
                        if jj >= 1:
                            v.wait_ge(s_kvr, jj)
                        for h in range(2):
                            v.wait_ge(s_ke, 4 * jj + 2 * (h + 1))
                            nc.vector.scalar_tensor_tensor(
                                ksb[:, h * 512:(h + 1) * 512], fma[:, h, :],
                                1.0, fme[:, h, :], Alu.add, Alu.min) \
                                .then_inc(s_kev, 1)
                        for ci in f_sub(M, j):
                            v_ffn_evac(v, ci)
                    if m_in_s == NMAC_S - 1:
                        v.wait_ge(s_kvr, SPM * (M + 1))
                        nc.vector.memzero(kvb[:, sq, :, :])
                        nc.vector.memzero(KSb[:, sq, :, :])
                        v.drain()
                        nc.vector.tensor_copy(kvb[0:64, sq, :, 0:64],
                                              kvp_ps[0:64, :, 0:64])
                        nc.vector.tensor_copy(kvb[64:128, sq, :, 64:128],
                                              kvp_ps[64:128, :, 64:128])
                        ks_fs = n_seq * KC * 16
                        nc.vector.tensor_copy(
                            bass.AP(KSb, sq * KC * 16, [[ks_fs, 64], [18, KC]]),
                            ks_ps[0:64, :])
                        nc.vector.tensor_copy(
                            bass.AP(KSb, 64 * ks_fs + sq * KC * 16 + 1,
                                    [[ks_fs, 64], [18, KC]]),
                            ks_ps[64:128, :]).then_inc(s_kvev, 1)
                        v.drain()
                    g_m = f_of(M)
                    for e in range(0, len(g_m), 2):
                        v_ffn_pair(v, g_m[e:e + 2])
                for ci in f_of(NMAC - 1):
                    v_ffn_evac(v, ci)

            @blk.scalar
            def _(s):
                s.wait_ge(s_cst, 16)
                for it in range(NMAC + 1):
                    if it < NMAC:
                        xb = xbuf[:, it % 2, :, :]
                        s.wait_ge(s_xld[it % 2], 16 * (it // 2 + 1))
                        if it >= 1:
                            s.wait_ge(s_vst, it)
                            s.drain()
                        for j in range(SPM):
                            nc.scalar.activation(p0v1[:, 0, :], xb[:, j, :],
                                                 Act.Square,
                                                 accum_out=stt1[:, 4 + j:5 + j]) \
                                .then_inc(s_ssq, 1)
                            s.drain()
                        s.wait_ge(s_vst, it + 1)
                        nc.scalar.activation(stt1[:, 20:20 + SPM],
                                             stt1[:, 8:8 + SPM],
                                             Act.Ln, bias=LN_EPS)
                        s.drain()
                        nc.scalar.activation(stt1[:, 16:16 + SPM],
                                             stt1[:, 20:20 + SPM],
                                             Act.Exp, scale=-0.5) \
                            .then_inc(s_srs, 1)
                    if it == 0:
                        for j in range(SPM):
                            s.wait_ge(s_tr, j + 1)
                            nc.scalar.copy(
                                xn1T[:, :, j * P:(j + 1) * P],
                                tr_ps[:].rearrange("p (c t) -> p c t", t=P)) \
                                .then_inc(s_trev, 1)
                        continue
                    M = it - 1
                    for ic in range(KC):
                        ps = q_ps0 if ic % 2 == 0 else q_ps1
                        if ic >= 2:
                            s.wait_ge(s_qev, 8 * M + ic - 1)
                        s.wait_ge(s_qmm, 8 * M + ic + 1)
                        nc.scalar.activation(fme[:, ic % 2, :], ps[:],
                                             Act.Exp).then_inc(s_qe, 1)
                        nc.scalar.activation(fma[:, ic % 2, :], ps[:],
                                             Act.Relu).then_inc(s_qe, 1)
                    for j in range(SPM):
                        jj = SPM * M + j
                        s.wait_ge(s_kmm, jj + 1)
                        if jj == SPM * M:
                            s.wait_ge(s_qev, 8 * M + 8)
                        if jj >= 1:
                            s.wait_ge(s_kev, 2 * jj)
                        for h in range(2):
                            nc.scalar.activation(
                                fme[:, h, :], k_ps[:, h * 512:(h + 1) * 512],
                                Act.Exp).then_inc(s_ke, 1)
                            nc.scalar.activation(
                                fma[:, h, :], k_ps[:, h * 512:(h + 1) * 512],
                                Act.Relu).then_inc(s_ke, 1)
                        s.wait_ge(s_vmm, jj + 1)
                        if jj >= 1:
                            s.wait_ge(s_kvr, jj)
                        nc.scalar.copy(vsb[:, 0:512], q_ps0[:])
                        nc.scalar.copy(vsb[:, 512:1024], q_ps1[:]) \
                            .then_inc(s_vev, 1)
                    if it < NMAC:
                        for j in range(SPM):
                            s.wait_ge(s_tr, 4 * it + j + 1)
                            nc.scalar.copy(
                                xn1T[:, :, j * P:(j + 1) * P],
                                tr_ps[:].rearrange("p (c t) -> p c t", t=P)) \
                                .then_inc(s_trev, 1)
                    g_m = f_of(M)
                    for e in range(0, len(g_m), 2):
                        fis = g_m[e:e + 2]
                        fa, fb = fis[0], fis[-1]
                        if fa >= 2:
                            s.wait_ge(s_q_u, fa - 1)
                            s.wait_ge(s_q_q, fa - 1)
                        Fs = []
                        for fi in fis:
                            which, i, pc, F = FCL[fi]
                            li = NA + fi
                            Fs.append(F)
                            s.wait_ge(s_wld[li % 2], 16 * (li // 2 + 1))
                            nc.scalar.activation(
                                p0ab[:, fi % 2, 0:F], wst[:, li % 2, 0:F],
                                Act.Abs, scale=6.0).then_inc(s_q_ab, 1)
                            nc.scalar.activation(
                                p0sg[:, fi % 2, 0:F], wst[:, li % 2, 0:F],
                                Act.Sign).then_inc(s_q_sg, 1)
                        s.wait_ge(s_q_u, fb + 1)
                        for fi, F in zip(fis, Fs):
                            nc.scalar.activation(p0v1[:, fi % 2, 0:F],
                                                 p0u[:, fi % 2, 0:F],
                                                 Act.Copy, bias=4.0)
                            nc.scalar.activation(p0v2[:, fi % 2, 0:F],
                                                 p0u[:, fi % 2, 0:F],
                                                 Act.Copy, bias=7.0)
                        s.drain()
                        for fi, F in zip(fis, Fs):
                            nc.scalar.activation(p0r[:, fi % 2, 0:F],
                                                 p0v1[:, fi % 2, 0:F],
                                                 Act.Relu, bias=6.0, scale=-1.0)
                            nc.scalar.activation(p0B[:, fi % 2, 0:F],
                                                 p0v2[:, fi % 2, 0:F],
                                                 Act.Relu, bias=-9.0) \
                                .then_inc(s_q_s, 1)

            @blk.tensor
            def _(te):
                te.wait_ge(s_id, 2)
                te.wait_ge(s_wcast, 24)
                for M in range(NMAC):
                    te.wait_ge(s_lnd, M + 1)
                    if M >= 2:
                        te.wait_ge(s_p0ev, NF * (M - 1) // NMAC)
                    for j in range(SPM):
                        if 4 * M + j >= 1:
                            te.wait_ge(s_trev, 4 * M + j)
                        ins = None
                        for c in range(KC):
                            ins = nc.tensor.transpose(
                                tr_ps[:, c * P:(c + 1) * P],
                                xn1[:, M % 2, j, c * P:(c + 1) * P], ident[:])
                        ins.then_inc(s_tr, 1)
                    te.wait_ge(s_trev, 4 * M + 4)
                    for ic in range(KC):
                        ps = q_ps0 if ic % 2 == 0 else q_ps1
                        if ic >= 2:
                            te.wait_ge(s_qe, 2 * (8 * M + ic - 1))
                        elif M >= 1:
                            te.wait_ge(s_vev, SPM * M)
                        ins = None
                        for c in range(KC):
                            ins = nc.tensor.matmul(
                                ps[:], wqT[:, c, ic * P:(ic + 1) * P],
                                xn1T[:, c, :], start=(c == 0), stop=(c == KC - 1))
                        ins.then_inc(s_qmm, 1)
                    for j in range(SPM):
                        jj = SPM * M + j
                        if jj >= 1:
                            te.wait_ge(s_ke, 4 * jj)
                        ins = None
                        for c in range(KC):
                            lhs = xn1T[:, c, j * P:(j + 1) * P]
                            nc.tensor.matmul(
                                k_ps[:, 0:512], lhs, wkT[:, c, 0:512],
                                start=(c == 0), stop=(c == KC - 1))
                            ins = nc.tensor.matmul(
                                k_ps[:, 512:1024], lhs, wkT[:, c, 512:1024],
                                start=(c == 0), stop=(c == KC - 1))
                        ins.then_inc(s_kmm, 1)
                        if j == 0:
                            te.wait_ge(s_qe, 2 * (8 * M + 8))
                        else:
                            te.wait_ge(s_vev, jj)
                        ins = None
                        for c in range(KC):
                            lhs = xn1T[:, c, j * P:(j + 1) * P]
                            nc.tensor.matmul(
                                q_ps0[:], lhs, wvT[:, c, 0:512],
                                start=(c == 0), stop=(c == KC - 1))
                            ins = nc.tensor.matmul(
                                q_ps1[:], lhs, wvT[:, c, 512:1024],
                                start=(c == 0), stop=(c == KC - 1))
                        ins.then_inc(s_vmm, 1)
                        te.wait_ge(s_kev, 2 * jj + 2)
                        te.wait_ge(s_vev, jj + 1)
                        m_in_s = M % NMAC_S
                        sq_t = M // NMAC_S
                        if j == 0 and m_in_s == 0 and sq_t > 0:
                            te.wait_ge(s_kvev, sq_t)
                        first = (m_in_s == 0 and j == 0)
                        last = (m_in_s == NMAC_S - 1 and j == SPM - 1)
                        for pr in range(KC):
                            lhs = ksb[:, pr * P:(pr + 1) * P]
                            st = (first and (pr == 0 or pr == 4))
                            sp = (last and (pr == 3 or pr == 7))
                            nc.tensor.matmul(
                                kvp_ps[:, pr, :], lhs,
                                vsb[:, pr * P:(pr + 1) * P],
                                start=st, stop=sp)
                            ins = nc.tensor.matmul(
                                ks_ps[:, pr:pr + 1], lhs, ones1[:],
                                start=(first and pr == 0),
                                stop=(last and pr == KC - 1))
                        ins.then_inc(s_kvr, 1)
                        for ci in f_sub(M, j):
                            F = FCL[ci][3]
                            te.wait_ge(s_q_q, ci + 1)
                            if ci >= 1:
                                te.wait_ge(s_p0ev, ci)
                            ins = None
                            for k in range(F // P):
                                ins = nc.tensor.transpose(
                                    tr_ps[:, k * P:(k + 1) * P],
                                    p0q[:, ci % 6, k * P:(k + 1) * P], ident[:])
                            ins.then_inc(s_p0tr, 1)
                for ci in f_of(NMAC - 1):
                    F = FCL[ci][3]
                    te.wait_ge(s_q_q, ci + 1)
                    if ci >= 1:
                        te.wait_ge(s_p0ev, ci)
                    ins = None
                    for k in range(F // P):
                        ins = nc.tensor.transpose(
                            tr_ps[:, k * P:(k + 1) * P],
                            p0q[:, ci % 6, k * P:(k + 1) * P], ident[:])
                    ins.then_inc(s_p0tr, 1)

        attn3_es.close()
        s_qld = [sem("s_qld0"), sem("s_qld1")]
        s_qfr = sem("s_qfr")
        s_xld2 = [sem("s_xld20"), sem("s_xld21")]
        s_nd = sem("s_nd")
        s_dv = sem("s_dv")
        s_otr = sem("s_otr")
        s_oev = sem("s_oev")
        s_wo = sem("s_wo")
        s_x2 = sem("s_x2")
        s_x2sp = [sem("s_x2sp0"), sem("s_x2sp1")]
        s_vst2 = sem("s_vst2")
        s_ssq2 = sem("s_ssq2")
        s_srs2 = sem("s_srs2")
        s_2ab = sem("s_2ab")
        s_2xc = sem("s_2xc")
        s_2u = sem("s_2u")
        s_2s = sem("s_2s")
        s_2sg = sem("s_2sg")
        s_2g = sem("s_2g")
        s_2q = sem("s_2q")
        s_xtr = sem("s_xtr")
        s_xev = sem("s_xev")
        s_xqsp = sem("s_xqsp")
        s_wol = [sem("s_wol0"), sem("s_wol1")]
        s_woc = sem("s_woc")

        NBD = DM // 16

        with ExitStack() as _pes:
            woT = _pes.enter_context(nc.sbuf_tensor("woT_sb", [P, KC, DM], BF16))
            wos = _pes.enter_context(nc.sbuf_tensor("wos", [P, 2, DM], F32))
            qmac = _pes.enter_context(nc.sbuf_tensor("qmac", [P, 2, KC, MACRO], BF16))
            xbuf2 = _pes.enter_context(nc.sbuf_tensor("xbuf2", [P, 2, SPM, DM], F32))
            dnsb = _pes.enter_context(nc.sbuf_tensor("dnsb", [P, 16], F32))
            rcp = _pes.enter_context(nc.sbuf_tensor("rcp", [P, 16], F32))
            osb = _pes.enter_context(nc.sbuf_tensor("osb", [P, 2, DM], BF16))
            oT = _pes.enter_context(nc.sbuf_tensor("oT", [P, 2, KC, P], BF16))
            x2m = _pes.enter_context(nc.sbuf_tensor("x2m", [P, SPM, DM], F32))
            stt2 = _pes.enter_context(nc.sbuf_tensor("stt2", [P, 24], F32))
            junk2 = _pes.enter_context(nc.sbuf_tensor("junk2", [P, DM], BF16))
            xcm = _pes.enter_context(nc.sbuf_tensor("xcm", [P, SPM, DM], F32))
            q2u = _pes.enter_context(nc.sbuf_tensor("q2u", [P, 2, DM], F32))
            q2ab = _pes.enter_context(nc.sbuf_tensor("q2ab", [P, 2, DM], F32))
            q2v1 = _pes.enter_context(nc.sbuf_tensor("q2v1", [P, 2, DM], FP8))
            q2v2 = _pes.enter_context(nc.sbuf_tensor("q2v2", [P, 2, DM], FP8))
            q2r = _pes.enter_context(nc.sbuf_tensor("q2r", [P, 2, DM], BF16))
            q2B = _pes.enter_context(nc.sbuf_tensor("q2B", [P, 2, DM], BF16))
            q2c2 = _pes.enter_context(nc.sbuf_tensor("q2c2", [P, 2, DM], BF16))
            q2g = _pes.enter_context(nc.sbuf_tensor("q2g", [P, 2, DM], BF16))
            q2sg = _pes.enter_context(nc.sbuf_tensor("q2sg", [P, 2, DM], BF16))
            q2q = _pes.enter_context(nc.sbuf_tensor("q2q", [P, SPM, DM], BF16))
            q2bm = _pes.enter_context(nc.sbuf_tensor("q2bm", [P, 2, NBD], F32))
            q2sp = _pes.enter_context(nc.sbuf_tensor("q2sp", [P, 2, NBD], F32))
            q2rr = _pes.enter_context(nc.sbuf_tensor("q2rr", [P, 2, NBD], F32))
            q2sb = _pes.enter_context(nc.sbuf_tensor("q2sb", [P, 2, NBD], BF16))
            xqT = _pes.enter_context(nc.sbuf_tensor("xqT", [P, SPM, KC, P], BF16))
            num_ps = _pes.enter_context(nc.psum_tensor("num_ps", [P, KC, P], F32))
            den_ps = _pes.enter_context(nc.psum_tensor("den_ps", [P, 16], F32))
            trA_ps = _pes.enter_context(nc.psum_tensor("trA_ps", [P, KC * P], BF16))
            trB_ps = _pes.enter_context(nc.psum_tensor("trB_ps", [P, KC * P], BF16))
            at_ps = _pes.enter_context(nc.psum_tensor("at_ps", [P, DM], F32))
            blk = _pes.enter_context(nc.Block())

            @blk.sync
            def _(sy):
                for c in range(KC):
                    if c >= 2:
                        sy.wait_ge(s_woc, c - 1)
                    sy.dma_start(wos[:, c % 2, :], woT_h[c * P:(c + 1) * P, :]) \
                        .then_inc(s_wol[c % 2], 16)
                for M in range(NMAC):
                    if M >= 2:
                        sy.wait_ge(s_qfr, M - 1)
                        sy.wait_ge(s_x2, SPM * (M - 1))
                    sy.dma_start(qmac[:, M % 2, :, :],
                                 q_s[M].rearrange("p (c t) -> p c t", t=MACRO)) \
                        .then_inc(s_qld[M % 2], 16)
                    sy.dma_start(
                        xbuf2[:, M % 2, :, :],
                        x_h[M * MACRO:(M + 1) * MACRO, :]
                            .rearrange("(j p) d -> p j d", p=P)) \
                        .then_inc(s_xld2[M % 2], 16)

            @blk.gpsimd
            def _(g):
                for M in range(NMAC):
                    for j in range(SPM):
                        U = SPM * M + j
                        g.wait_ge(s_x2, U + 1)
                        g.dma_start(x2_s[U * SUB:(U + 1) * SUB, :],
                                    x2m[:, j, :]).then_inc(s_x2sp[U % 2], 16)
                    for j in range(SPM):
                        U = SPM * M + j
                        g.wait_ge(s_xev, SPM * M + j + 1)
                        g.dma_start(xq_s[U],
                                    xqT[:, j, :, :].rearrange("p c t -> p (c t)")) \
                            .then_inc(s_xqsp, 16)
                g.wait_ge(s_x2sp[0], 16 * ((NSUB + 1) // 2))
                g.wait_ge(s_x2sp[1], 16 * (NSUB // 2))
                g.wait_ge(s_xqsp, 16 * NSUB)

            @blk.tensor
            def _(te):
                def o_tail(Uo):
                    jo = Uo % SPM
                    te.wait_ge(s_dv, Uo + 1)
                    ins = None
                    for c in range(KC):
                        ins = nc.tensor.transpose(
                            trA_ps[:, c * P:(c + 1) * P],
                            osb[:, Uo % 2, c * P:(c + 1) * P], ident[:])
                        if jo == SPM - 1 and c == KC - 2:
                            ins.then_inc(s_qfr, 1)
                    ins.then_inc(s_otr, 1)
                    te.wait_ge(s_oev, Uo + 1)
                    if Uo == 0:
                        te.wait_ge(s_woc, KC)
                    if Uo >= 1:
                        te.wait_ge(s_x2, Uo)
                    ins = None
                    for c in range(KC):
                        lhs = oT[:, Uo % 2, c, :]
                        nc.tensor.matmul(at_ps[:, 0:512], lhs,
                                         woT[:, c, 0:512],
                                         start=(c == 0), stop=(c == KC - 1))
                        ins = nc.tensor.matmul(at_ps[:, 512:1024], lhs,
                                               woT[:, c, 512:1024],
                                               start=(c == 0),
                                               stop=(c == KC - 1))
                    ins.then_inc(s_wo, 1)

                for it in range(NMAC + 1):
                    if it >= 1:
                        Mp = it - 1
                        for j in range(SPM):
                            U = SPM * Mp + j
                            te.wait_ge(s_2q, U + 1)
                            if U >= 1:
                                te.wait_ge(s_xev, U)
                            ins = None
                            for c in range(KC):
                                ins = nc.tensor.transpose(
                                    trB_ps[:, c * P:(c + 1) * P],
                                    q2q[:, j, c * P:(c + 1) * P], ident[:])
                            ins.then_inc(s_xtr, 1)
                    if it < NMAC:
                        M = it
                        sq = M // NMAC_S
                        te.wait_ge(s_qld[M % 2], 16 * (M // 2 + 1))
                        for j in range(SPM):
                            U = SPM * M + j
                            if U >= 1:
                                te.wait_ge(s_dv, U)
                            ins = None
                            for c in range(KC):
                                lhs = qmac[:, M % 2, c, j * P:(j + 1) * P]
                                nc.tensor.matmul(
                                    num_ps[:, c, :], lhs, kvb[:, sq, c, :],
                                    start=True, stop=True)
                                ins = nc.tensor.matmul(
                                    den_ps[:], lhs, KSb[:, sq, c, :],
                                    start=(c == 0), stop=(c == KC - 1))
                            ins.then_inc(s_nd, 1)
                            o_tail(U)

            @blk.vector
            def _(v):
                for c in range(KC):
                    v.wait_ge(s_wol[c % 2], 16 * (c // 2 + 1))
                    nc.vector.tensor_copy(woT[:, c, :], wos[:, c % 2, :]) \
                        .then_inc(s_woc, 1)
                for M in range(NMAC):
                    mcol = stt2[:, 0:SPM]
                    sscol = stt2[:, 4:4 + SPM]
                    vcol = stt2[:, 8:8 + SPM]
                    m2col = stt2[:, 12:12 + SPM]
                    rcol = stt2[:, 16:16 + SPM]
                    v.wait_ge(s_xld2[M % 2], 16 * (M // 2 + 1))
                    for j in range(SPM):
                        U = SPM * M + j
                        v.wait_ge(s_nd, U + 1)
                        nc.vector.tensor_scalar(dnsb[:], den_ps[:], ATT_EPS,
                                                None, Alu.add)
                        v.drain()
                        nc.vector.reciprocal(rcp[:], dnsb[:])
                        v.drain()
                        if U >= 2:
                            v.wait_ge(s_otr, U - 1)
                        nc.vector.tensor_tensor(
                            osb[:, U % 2, :].rearrange("p (a b) -> p a b", b=64),
                            num_ps[:].rearrange("p c t -> p (c t)")
                                .rearrange("p (a b) -> p a b", b=64),
                            rcp[:, :, None].broadcast_to([P, 16, 64]),
                            Alu.mult).then_inc(s_dv, 1)
                        v.wait_ge(s_wo, U + 1)
                        if M >= 1 and j == 0:
                            v.wait_ge(s_x2sp[0], 16 * (SPM * M // 2))
                            v.wait_ge(s_x2sp[1], 16 * (SPM * M // 2))
                            v.wait_ge(s_ssq2, SPM * M)
                        nc.vector.tensor_add(x2m[:, j, :], at_ps[:],
                                             xbuf2[:, M % 2, j, :]) \
                            .then_inc(s_x2, 1)
                    v.drain()
                    nc.vector.tensor_reduce(mcol[:, :, None], x2m[:], AxX, Alu.add)
                    v.drain()
                    nc.vector.tensor_scalar(mcol, mcol, 1.0 / DM, None, Alu.mult)
                    v.drain()
                    nc.vector.tensor_mul(m2col, mcol, mcol)
                    v.drain()
                    v.wait_ge(s_ssq2, SPM * (M + 1))
                    nc.vector.scalar_tensor_tensor(
                        vcol, sscol, 1.0 / DM, m2col, Alu.mult, Alu.subtract) \
                        .then_inc(s_vst2, 1)
                    if M >= 1:
                        v.wait_ge(s_2sg, SPM * M)
                    for j in range(SPM):
                        nc.vector.tensor_scalar(xcm[:, j, :], x2m[:, j, :],
                                                mcol[:, j:j + 1], None,
                                                Alu.subtract).then_inc(s_2xc, 1)
                    v.drain()
                    v.wait_ge(s_srs2, M + 1)
                    for j0 in range(0, SPM, 2):
                        for j in (j0, j0 + 1):
                            sl = j % 2
                            nc.vector.tensor_reduce(
                                q2bm[:, sl, :, None],
                                xcm[:, j, :].rearrange("p (b k) -> p b k", k=16),
                                AxX, Alu.max, apply_absolute_value=True)
                        v.drain()
                        for j in (j0, j0 + 1):
                            sl = j % 2
                            nc.vector.tensor_scalar(
                                q2sp[:, sl, :], q2bm[:, sl, :],
                                rcol[:, j:j + 1], 1.0 / 6.0, Alu.mult, Alu.mult)
                        v.drain()
                        for j in (j0, j0 + 1):
                            sl = j % 2
                            nc.vector.tensor_scalar(q2sp[:, sl, :], q2sp[:, sl, :],
                                                    QCLAMP, None, Alu.max)
                        v.drain()
                        for j in (j0, j0 + 1):
                            sl = j % 2
                            nc.vector.reciprocal(q2rr[:, sl, :], q2sp[:, sl, :])
                            nc.vector.tensor_copy(q2sb[:, sl, :], q2sp[:, sl, :])
                        v.drain()
                        for j in (j0, j0 + 1):
                            sl = j % 2
                            nc.vector.tensor_scalar(q2rr[:, sl, :], q2rr[:, sl, :],
                                                    rcol[:, j:j + 1], None,
                                                    Alu.mult)
                        v.drain()
                        for j in (j0, j0 + 1):
                            U = SPM * M + j
                            sl = j % 2
                            v.wait_ge(s_2ab, U + 1)
                            if U >= 2:
                                v.wait_ge(s_2s, U - 1)
                            nc.vector.tensor_tensor(
                                q2u[:, sl, :].rearrange("p (b k) -> p b k", k=16),
                                q2ab[:, sl, :].rearrange("p (b k) -> p b k", k=16),
                                q2rr[:, sl, :, None].broadcast_to([P, NBD, 16]),
                                Alu.mult).then_inc(s_2u, 1)
                        v.drain()
                        for j in (j0, j0 + 1):
                            sl = j % 2
                            nc.vector.tensor_scalar(q2c2[:, sl, :], q2u[:, sl, :],
                                                    5.0, 2.0, Alu.is_gt, Alu.mult)
                        for j in (j0, j0 + 1):
                            U = SPM * M + j
                            sl = j % 2
                            v.wait_ge(s_2s, U + 1)
                            nc.vector.scalar_tensor_tensor(
                                q2g[:, sl, :], q2B[:, sl, :], 2.0, q2r[:, sl, :],
                                Alu.min, Alu.subtract)
                        v.drain()
                        for j in (j0, j0 + 1):
                            sl = j % 2
                            nc.vector.scalar_tensor_tensor(
                                q2g[:, sl, :], q2c2[:, sl, :], 2.0, q2g[:, sl, :],
                                Alu.add, Alu.add).then_inc(s_2g, 1)
                        v.drain()
                        for j in (j0, j0 + 1):
                            U = SPM * M + j
                            sl = j % 2
                            v.wait_ge(s_2sg, U + 1)
                            nc.vector.tensor_mul(q2g[:, sl, :], q2g[:, sl, :],
                                                 q2sg[:, sl, :])
                        v.drain()
                        for j in (j0, j0 + 1):
                            U = SPM * M + j
                            sl = j % 2
                            if M >= 1 and j == 0:
                                v.wait_ge(s_xtr, SPM * M)
                            nc.vector.tensor_tensor(
                                q2q[:, j, :].rearrange("p (b k) -> p b k", k=16),
                                q2g[:, sl, :].rearrange("p (b k) -> p b k", k=16),
                                q2sb[:, sl, :, None].broadcast_to([P, NBD, 16]),
                                Alu.mult).then_inc(s_2q, 1)

            @blk.scalar
            def _(s):
                for M in range(NMAC):
                    for j in range(SPM if M >= 1 else 0):
                        U = SPM * (M - 1) + j
                        s.wait_ge(s_xtr, U + 1)
                        if U >= 1:
                            s.wait_ge(s_xqsp, 16 * U)
                        nc.scalar.copy(
                            xqT[:, j, :, :].rearrange("p c t -> p (c t)"),
                            trB_ps[:]).then_inc(s_xev, 1)
                    for j in range(SPM):
                        U = SPM * M + j
                        s.wait_ge(s_otr, U + 1)
                        if U >= 2:
                            s.wait_ge(s_wo, U - 1)
                        nc.scalar.copy(oT[:, U % 2, :, :]
                                       .rearrange("p c t -> p (c t)"),
                                       trA_ps[:]).then_inc(s_oev, 1)
                        s.wait_ge(s_x2, U + 1)
                        nc.scalar.activation(junk2[:], x2m[:, j, :], Act.Square,
                                             accum_out=stt2[:, 4 + j:5 + j]) \
                            .then_inc(s_ssq2, 1)
                        s.drain()
                    s.wait_ge(s_vst2, M + 1)
                    nc.scalar.activation(stt2[:, 20:20 + SPM], stt2[:, 8:8 + SPM],
                                         Act.Ln, bias=LN_EPS)
                    s.drain()
                    nc.scalar.activation(stt2[:, 16:16 + SPM],
                                         stt2[:, 20:20 + SPM],
                                         Act.Exp, scale=-0.5).then_inc(s_srs2, 1)
                    for j in range(SPM):
                        U = SPM * M + j
                        sl = j % 2
                        s.wait_ge(s_2xc, U + 1)
                        if U >= 2:
                            s.wait_ge(s_2u, U - 1)
                        nc.scalar.activation(q2ab[:, sl, :], xcm[:, j, :],
                                             Act.Abs).then_inc(s_2ab, 1)
                        if U >= 2:
                            s.wait_ge(s_2q, U - 1)
                        nc.scalar.activation(q2sg[:, sl, :], xcm[:, j, :],
                                             Act.Sign).then_inc(s_2sg, 1)
                        s.wait_ge(s_2u, U + 1)
                        if U >= 2:
                            s.wait_ge(s_2g, U - 1)
                        nc.scalar.activation(q2v1[:, sl, :], q2u[:, sl, :],
                                             Act.Copy, bias=4.0)
                        nc.scalar.activation(q2v2[:, sl, :], q2u[:, sl, :],
                                             Act.Copy, bias=7.0)
                        s.drain()
                        nc.scalar.activation(q2r[:, sl, :], q2v1[:, sl, :],
                                             Act.Relu, bias=6.0, scale=-1.0)
                        nc.scalar.activation(q2B[:, sl, :], q2v2[:, sl, :],
                                             Act.Relu, bias=-9.0).then_inc(s_2s, 1)
                if True:
                    for j in range(SPM):
                        U = SPM * (NMAC - 1) + j
                        s.wait_ge(s_xtr, U + 1)
                        s.wait_ge(s_xqsp, 16 * U)
                        nc.scalar.copy(
                            xqT[:, j, :, :].rearrange("p c t -> p (c t)"),
                            trB_ps[:]).then_inc(s_xev, 1)

        wo_es.close()
        s_wres = sem("s_wres")
        s_x2l = [sem("s_x2l0"), sem("s_x2l1")]
        s_xql = [sem("s_xql0"), sem("s_xql1")]
        s_xqf = sem("s_xqf")
        s_gu = sem("s_gu")
        s_sil = sem("s_sil")
        s_h = sem("s_h")
        s_3ab = sem("s_3ab")
        s_3u = sem("s_3u")
        s_3s = sem("s_3s")
        s_3sg = sem("s_3sg")
        s_3g = sem("s_3g")
        s_3q = sem("s_3q")
        s_3tr = sem("s_3tr")
        s_3tev = sem("s_3tev")
        s_dnr = sem("s_dnr")
        s_wdres = sem("s_wdres")
        s_x3 = sem("s_x3")
        s_yst = [sem("s_yst0"), sem("s_yst1")]

        QCH = 704
        NQC = DF // QCH
        QNB = QCH // 16
        WAVES = [(0, 8), (8, 8), (16, 6)]
        CH_SLAB = [min(NSL - 1, (QCH * (c + 1) - 1) // 512) for c in range(NQC)]
        WV_CH = [min(NQC - 1, (P * (w0 + n) - 1) // QCH) for w0, n in WAVES]

        with ExitStack() as _pes:
            qwg = _pes.enter_context(nc.sbuf_tensor("qwg", [P, KC, DF], BF16))
            qwu = _pes.enter_context(nc.sbuf_tensor("qwu", [P, KC, DF], BF16))
            qwd = _pes.enter_context(nc.sbuf_tensor("qwd", [P, NCH, DM], BF16))
            x2b = _pes.enter_context(nc.sbuf_tensor("x2b", [P, 2, DM], F32))
            xqb = _pes.enter_context(nc.sbuf_tensor("xqb", [P, 2, KC, P], BF16))
            sgsb = _pes.enter_context(nc.sbuf_tensor("sgsb", [P, 512], F32))
            hsb = _pes.enter_context(nc.sbuf_tensor("hsb", [P, 2, DF], F32))
            q3u = _pes.enter_context(nc.sbuf_tensor("q3u", [P, 2, QCH], F32))
            q3ab = _pes.enter_context(nc.sbuf_tensor("q3ab", [P, 2, QCH], F32))
            q3v1 = _pes.enter_context(nc.sbuf_tensor("q3v1", [P, 2, QCH], FP8))
            q3v2 = _pes.enter_context(nc.sbuf_tensor("q3v2", [P, 2, QCH], FP8))
            q3r = _pes.enter_context(nc.sbuf_tensor("q3r", [P, 2, QCH], BF16))
            q3B = _pes.enter_context(nc.sbuf_tensor("q3B", [P, 2, QCH], BF16))
            q3c2 = _pes.enter_context(nc.sbuf_tensor("q3c2", [P, 2, QCH], BF16))
            q3g = _pes.enter_context(nc.sbuf_tensor("q3g", [P, 2, QCH], BF16))
            q3sg = _pes.enter_context(nc.sbuf_tensor("q3sg", [P, 2, QCH], BF16))
            q3q = _pes.enter_context(nc.sbuf_tensor("q3q", [P, DF], BF16))
            q3bm = _pes.enter_context(nc.sbuf_tensor("q3bm", [P, 2, QNB], F32))
            q3rsi = _pes.enter_context(nc.sbuf_tensor("q3rsi", [P, 2, QNB], F32))
            q3sb = _pes.enter_context(nc.sbuf_tensor("q3sb", [P, 2, QNB], BF16))
            qhT = _pes.enter_context(nc.sbuf_tensor("qhT", [P, NCH, P], BF16))
            gu_psA = _pes.enter_context(nc.psum_tensor("gu_psA", [P, DM], F32))
            gu_psB = _pes.enter_context(nc.psum_tensor("gu_psB", [P, DM], F32))
            dn_ps = _pes.enter_context(nc.psum_tensor("dn_ps", [P, DM], F32))
            qh_psA = _pes.enter_context(nc.psum_tensor("qh_psA", [P, 1024], BF16))
            qh_psB = _pes.enter_context(nc.psum_tensor("qh_psB", [P, 1024], BF16))
            blk = _pes.enter_context(nc.Block())

            @blk.sync
            def _(sy):
                sy.dma_start(qwg[:], qwg_s[:].rearrange("c p f -> p c f")) \
                    .then_inc(s_wres, 16)
                sy.dma_start(qwu[:], qwu_s[:].rearrange("c p f -> p c f")) \
                    .then_inc(s_wres, 16)
                sy.dma_start(qwd[:], qwd_s[:].rearrange("c p f -> p c f")) \
                    .then_inc(s_wdres, 16)
                for U in range(NSUB):
                    if U >= 2:
                        sy.wait_ge(s_xqf, U - 1)
                        sy.wait_ge(s_yst[U % 2], 16 * (U // 2))
                    sy.dma_start(x2b[:, U % 2, :], x2_s[U * SUB:(U + 1) * SUB, :]) \
                        .then_inc(s_x2l[U % 2], 16)
                    sy.dma_start(xqb[:, U % 2, :, :],
                                 xq_s[U].rearrange("p (c t) -> p c t", t=P)) \
                        .then_inc(s_xql[U % 2], 16)

            @blk.gpsimd
            def _(g):
                for U in range(NSUB):
                    g.wait_ge(s_x3, 2 * (U + 1))
                    g.dma_start(y_h[U * SUB:(U + 1) * SUB, :], x2b[:, U % 2, :]) \
                        .then_inc(s_yst[U % 2], 16)
                g.wait_ge(s_yst[0], 16 * ((NSUB + 1) // 2))
                g.wait_ge(s_yst[1], 16 * (NSUB // 2))

            @blk.tensor
            def _(te):
                te.wait_ge(s_wres, 32)
                for it in range(NSUB + 1):
                    if it < NSUB:
                        U = it
                        te.wait_ge(s_xql[U % 2], 16 * (U // 2 + 1))
                        for l, (n0, w) in enumerate(SLABS):
                            gu = gu_psA if l % 2 == 0 else gu_psB
                            li = NSL * U + l
                            if li >= 2:
                                te.wait_ge(s_h, li - 1)
                            ins = None
                            for c in range(KC):
                                lhs = xqb[:, U % 2, c, :]
                                nc.tensor.matmul(
                                    gu[:, 0:w], lhs, qwg[:, c, n0:n0 + w],
                                    start=(c == 0), stop=(c == KC - 1))
                                ins = nc.tensor.matmul(
                                    gu[:, 512:512 + w], lhs, qwu[:, c, n0:n0 + w],
                                    start=(c == 0), stop=(c == KC - 1))
                            ins.then_inc(s_gu, 1)
                    if it >= 1:
                        U = it - 1
                        for w_i, (p0, npc) in enumerate(WAVES):
                            qh_ps = qh_psA if w_i % 2 == 0 else qh_psB
                            te.wait_ge(s_3q, NQC * U + WV_CH[w_i] + 1)
                            if w_i == 2:
                                te.wait_ge(s_3tev, 3 * U + 1)
                            elif U >= 1:
                                te.wait_ge(s_3tev, 3 * U - w_i)
                            ins = None
                            for k in range(npc):
                                ins = nc.tensor.transpose(
                                    qh_ps[:, k * P:(k + 1) * P],
                                    q3q[:, (p0 + k) * P:(p0 + k + 1) * P],
                                    ident[:])
                                if w_i == 0 and k == 0:
                                    ins.then_inc(s_xqf, 1)
                            ins.then_inc(s_3tr, 1)
                        te.wait_ge(s_3tev, 3 * (U + 1))
                        if U == 0:
                            te.wait_ge(s_wdres, 16)
                        if U >= 1:
                            te.wait_ge(s_x3, 2 * U)
                        for hf in range(2):
                            ins = None
                            for n_c in range(NCH):
                                ins = nc.tensor.matmul(
                                    dn_ps[:, hf * 512:(hf + 1) * 512],
                                    qhT[:, n_c, :],
                                    qwd[:, n_c, hf * 512:(hf + 1) * 512],
                                    start=(n_c == 0), stop=(n_c == NCH - 1))
                            ins.then_inc(s_dnr, 1)

            @blk.scalar
            def _(s):
                for U in range(NSUB):
                    for l, (n0, w) in enumerate(SLABS):
                        gu = gu_psA if l % 2 == 0 else gu_psB
                        li = NSL * U + l
                        s.wait_ge(s_gu, li + 1)
                        if li >= 1:
                            s.wait_ge(s_h, li)
                        nc.scalar.activation(sgsb[:, 0:w], gu[:, 0:w], Act.Silu) \
                            .then_inc(s_sil, 1)
                    for w_i, (p0, npc) in enumerate(WAVES if U >= 1 else []):
                        qh_ps = qh_psA if w_i % 2 == 0 else qh_psB
                        s.wait_ge(s_3tr, 3 * (U - 1) + w_i + 1)
                        if U >= 2 and w_i == 0:
                            s.wait_ge(s_dnr, 2 * (U - 1))
                        nc.scalar.copy(
                            qhT[:, p0:p0 + npc, :].rearrange("p c t -> p (c t)"),
                            qh_ps[:, 0:npc * P]).then_inc(s_3tev, 1)

                    for c0 in range(0, NQC, 2):
                        for c in (c0, c0 + 1):
                            ci = NQC * U + c
                            sl = c % 2
                            hv = hsb[:, U % 2, QCH * c:QCH * (c + 1)]
                            if ci >= 2:
                                s.wait_ge(s_3u, ci - 1)
                            s.wait_ge(s_h, NSL * U + CH_SLAB[c] + 1)
                            nc.scalar.activation(q3ab[:, sl, :], hv, Act.Abs,
                                                 scale=6.0).then_inc(s_3ab, 1)
                            if ci >= 2:
                                s.wait_ge(s_3q, ci - 1)
                            nc.scalar.activation(q3sg[:, sl, :], hv, Act.Sign) \
                                .then_inc(s_3sg, 1)
                        for c in (c0, c0 + 1):
                            ci = NQC * U + c
                            sl = c % 2
                            s.wait_ge(s_3u, ci + 1)
                            if ci >= 2:
                                s.wait_ge(s_3g, ci - 1)
                            nc.scalar.activation(q3v1[:, sl, :], q3u[:, sl, :],
                                                 Act.Copy, bias=4.0)
                            nc.scalar.activation(q3v2[:, sl, :], q3u[:, sl, :],
                                                 Act.Copy, bias=7.0)
                        s.drain()
                        for c in (c0, c0 + 1):
                            sl = c % 2
                            nc.scalar.activation(q3r[:, sl, :], q3v1[:, sl, :],
                                                 Act.Relu, bias=6.0, scale=-1.0)
                            nc.scalar.activation(q3B[:, sl, :], q3v2[:, sl, :],
                                                 Act.Relu, bias=-9.0) \
                                .then_inc(s_3s, 1)
                for w_i, (p0, npc) in enumerate(WAVES):
                    qh_ps = qh_psA if w_i % 2 == 0 else qh_psB
                    s.wait_ge(s_3tr, 3 * (NSUB - 1) + w_i + 1)
                    if True and w_i == 0:
                        s.wait_ge(s_dnr, 2 * (NSUB - 1))
                    nc.scalar.copy(
                        qhT[:, p0:p0 + npc, :].rearrange("p c t -> p (c t)"),
                        qh_ps[:, 0:npc * P]).then_inc(s_3tev, 1)

            @blk.vector
            def _(v):
                for U in range(NSUB):
                    v.wait_ge(s_x2l[U % 2], 16 * (U // 2 + 1))
                    for l, (n0, w) in enumerate(SLABS):
                        gu = gu_psA if l % 2 == 0 else gu_psB
                        li = NSL * U + l
                        v.wait_ge(s_sil, li + 1)
                        if U >= 2 and l == 0:
                            v.wait_ge(s_3sg, NQC * (U - 1))
                        nc.vector.tensor_tensor(
                            hsb[:, U % 2, n0:n0 + w], sgsb[:, 0:w],
                            gu[:, 512:512 + w], Alu.mult).then_inc(s_h, 1)
                    for c0 in range(0, NQC, 2):
                        for c in (c0, c0 + 1):
                            sl = c % 2
                            hv = hsb[:, U % 2, QCH * c:QCH * (c + 1)]
                            nc.vector.tensor_reduce(
                                q3bm[:, sl, :, None],
                                hv.rearrange("p (b k) -> p b k", k=16),
                                AxX, Alu.max, apply_absolute_value=True)
                        v.drain()
                        for c in (c0, c0 + 1):
                            sl = c % 2
                            nc.vector.reciprocal(q3rsi[:, sl, :], q3bm[:, sl, :])
                            nc.vector.tensor_scalar(
                                q3sb[:, sl, :], q3bm[:, sl, :], 1.0 / 6.0,
                                QCLAMP, Alu.mult, Alu.max)
                        v.drain()
                        for c in (c0, c0 + 1):
                            ci = NQC * U + c
                            sl = c % 2
                            v.wait_ge(s_3ab, ci + 1)
                            if ci >= 2:
                                v.wait_ge(s_3s, ci - 1)
                            nc.vector.tensor_tensor(
                                q3u[:, sl, :].rearrange("p (b k) -> p b k", k=16),
                                q3ab[:, sl, :].rearrange("p (b k) -> p b k", k=16),
                                q3rsi[:, sl, :, None].broadcast_to([P, QNB, 16]),
                                Alu.mult).then_inc(s_3u, 1)
                        v.drain()
                        for c in (c0, c0 + 1):
                            sl = c % 2
                            nc.vector.tensor_scalar(q3c2[:, sl, :], q3u[:, sl, :],
                                                    5.0, 2.0, Alu.is_gt, Alu.mult)
                        for c in (c0, c0 + 1):
                            ci = NQC * U + c
                            sl = c % 2
                            v.wait_ge(s_3s, ci + 1)
                            nc.vector.scalar_tensor_tensor(
                                q3g[:, sl, :], q3B[:, sl, :], 2.0, q3r[:, sl, :],
                                Alu.min, Alu.subtract)
                        v.drain()
                        for c in (c0, c0 + 1):
                            sl = c % 2
                            nc.vector.scalar_tensor_tensor(
                                q3g[:, sl, :], q3c2[:, sl, :], 2.0, q3g[:, sl, :],
                                Alu.add, Alu.add).then_inc(s_3g, 1)
                        v.drain()
                        for c in (c0, c0 + 1):
                            ci = NQC * U + c
                            sl = c % 2
                            v.wait_ge(s_3sg, ci + 1)
                            nc.vector.tensor_mul(q3g[:, sl, :], q3g[:, sl, :],
                                                 q3sg[:, sl, :])
                        v.drain()
                        for c in (c0, c0 + 1):
                            ci = NQC * U + c
                            sl = c % 2
                            if U >= 1 and c == 0:
                                v.wait_ge(s_3tr, 3 * U)
                            nc.vector.tensor_tensor(
                                q3q[:, QCH * c:QCH * (c + 1)]
                                    .rearrange("p (b k) -> p b k", k=16),
                                q3g[:, sl, :].rearrange("p (b k) -> p b k", k=16),
                                q3sb[:, sl, :, None].broadcast_to([P, QNB, 16]),
                                Alu.mult).then_inc(s_3q, 1)
                    if U >= 1:
                        for hf in range(2):
                            v.wait_ge(s_dnr, 2 * (U - 1) + hf + 1)
                            nc.vector.tensor_add(
                                x2b[:, (U - 1) % 2, hf * 512:(hf + 1) * 512],
                                dn_ps[:, hf * 512:(hf + 1) * 512],
                                x2b[:, (U - 1) % 2, hf * 512:(hf + 1) * 512]) \
                                .then_inc(s_x3, 1)
                for hf in range(2):
                    v.wait_ge(s_dnr, 2 * (NSUB - 1) + hf + 1)
                    nc.vector.tensor_add(
                        x2b[:, (NSUB - 1) % 2, hf * 512:(hf + 1) * 512],
                        dn_ps[:, hf * 512:(hf + 1) * 512],
                        x2b[:, (NSUB - 1) % 2, hf * 512:(hf + 1) * 512]) \
                        .then_inc(s_x3, 1)

    return nc


def kernel(L0_attn_x, L0_attn_Wq, L0_attn_Wk, L0_attn_Wv, L0_attn_Wo,
           L0_ffn_w_gate, L0_ffn_w_up, L0_ffn_w_down):
    x = np.ascontiguousarray(np.asarray(L0_attn_x, dtype=np.float32))
    B, N, D = x.shape
    n_cores = 8
    per = B // n_cores
    wqT = np.ascontiguousarray(np.asarray(L0_attn_Wq, np.float32).T)
    wkT = np.ascontiguousarray(np.asarray(L0_attn_Wk, np.float32).T)
    wvT = np.ascontiguousarray(np.asarray(L0_attn_Wv, np.float32).T)
    woT = np.ascontiguousarray(np.asarray(L0_attn_Wo, np.float32).T)
    wg = np.ascontiguousarray(np.asarray(L0_ffn_w_gate, np.float32))
    wu = np.ascontiguousarray(np.asarray(L0_ffn_w_up, np.float32))
    wd = np.ascontiguousarray(np.asarray(L0_ffn_w_down, np.float32))

    nc = build_program(seq_len=N, n_seq=per)
    cstv = np.ascontiguousarray(np.broadcast_to(
        np.array([4.0, 7.0, 6.0, -1.0, -9.0, LN_EPS, -0.5, 1.0, 0.0],
                 np.float32), (128, 9)))
    in_maps = []
    for i in range(n_cores):
        in_maps.append({
            "x": np.ascontiguousarray(x[i * per:(i + 1) * per].reshape(per * N, D)),
            "wqT": wqT, "wkT": wkT, "wvT": wvT, "woT": woT,
            "wg": wg, "wu": wu, "wd": wd, "cst": cstv,
        })
    res = run_bass_kernel_spmd(nc, in_maps, core_ids=list(range(n_cores)))
    out = np.empty((B, N, D), np.float32)
    for i in range(n_cores):
        out[i * per:(i + 1) * per] = res.results[i]["y"].reshape(per, N, D)
    return out


# version 20
# speedup vs baseline: 1.1504x; 1.1504x over previous
import os
import numpy as np
from contextlib import ExitStack
import concourse.bass as bass
import concourse.mybir as mybir
from concourse.bass_utils import run_bass_kernel_spmd

F32 = mybir.dt.float32
BF16 = mybir.dt.bfloat16
FP8 = mybir.dt.float8e4
Alu = mybir.AluOpType
Act = mybir.ActivationFunctionType
AxX = mybir.AxisListType.X

P = 128
DM = 1024
DF = 2816
KC = DM // P
NCH = DF // P
MACRO = 512
SUB = 128
SPM = MACRO // SUB
LN_EPS = 1e-5
ATT_EPS = 1e-6
QCLAMP = 1e-12

SLABS = [(i * 512, min(512, DF - i * 512)) for i in range((DF + 511) // 512)]
NSL = len(SLABS)
PHASES = int(os.environ.get("KPH", "4"))
L1 = int(os.environ.get("KL1", "9"))


def build_program(seq_len=4096, n_seq=2):
    T = seq_len * n_seq
    NMAC = T // MACRO
    NMAC_S = seq_len // MACRO
    NSUB = T // SUB

    nc = bass.Bass("TRN2", target_bir_lowering=False)

    x_h = nc.dram_tensor("x", [T, DM], F32, kind="ExternalInput").ap()
    wqT_h = nc.dram_tensor("wqT", [DM, DM], F32, kind="ExternalInput").ap()
    wkT_h = nc.dram_tensor("wkT", [DM, DM], F32, kind="ExternalInput").ap()
    wvT_h = nc.dram_tensor("wvT", [DM, DM], F32, kind="ExternalInput").ap()
    woT_h = nc.dram_tensor("woT", [DM, DM], F32, kind="ExternalInput").ap()
    wg_h = nc.dram_tensor("wg", [DF, DM], F32, kind="ExternalInput").ap()
    wu_h = nc.dram_tensor("wu", [DF, DM], F32, kind="ExternalInput").ap()
    wd_h = nc.dram_tensor("wd", [DM, DF], F32, kind="ExternalInput").ap()
    y_h = nc.dram_tensor("y", [T, DM], F32, kind="ExternalOutput").ap()
    q_s = nc.dram_tensor("q_s", [NMAC, P, KC * MACRO], BF16).ap()
    x2_s = nc.dram_tensor("x2_s", [T, DM], F32).ap()
    xq_s = nc.dram_tensor("xq_s", [NSUB, P, KC * SUB], BF16).ap()
    qwg_s = nc.dram_tensor("qwg_s", [KC, P, DF], BF16).ap()
    qwu_s = nc.dram_tensor("qwu_s", [KC, P, DF], BF16).ap()
    qwd_s = nc.dram_tensor("qwd_s", [NCH, P, DM], BF16).ap()
    CONSTS = [4.0, 7.0, 6.0, -1.0, -9.0, float(LN_EPS), -0.5, 1.0, 0.0]
    cst_h = nc.dram_tensor("cst", [P, len(CONSTS)], F32, kind="ExternalInput").ap()

    es = ExitStack()
    with es:
        E = es.enter_context
        ident = E(nc.sbuf_tensor("ident", [P, P], BF16))
        ones1 = E(nc.sbuf_tensor("ones1", [P, 1], BF16))

        cst = E(nc.sbuf_tensor("cst_sb", [P, len(CONSTS)], F32))
        for _ci, _cv in enumerate(CONSTS):
            nc.const_aps.register(F32, _cv, cst[:, _ci:_ci + 1])
        wo_es = ExitStack()
        kvb = wo_es.enter_context(nc.sbuf_tensor("kvb", [P, n_seq, KC, P], BF16))
        KSb = wo_es.enter_context(nc.sbuf_tensor("KSb", [P, n_seq, KC, 16], BF16))
        attn3_es = ExitStack()
        wqT = attn3_es.enter_context(nc.sbuf_tensor("wqT_sb", [P, KC, DM], BF16))
        wkT = attn3_es.enter_context(nc.sbuf_tensor("wkT_sb", [P, KC, DM], BF16))
        wvT = attn3_es.enter_context(nc.sbuf_tensor("wvT_sb", [P, KC, DM], BF16))

        def sem(name):
            return E(nc.semaphore(name))

        ACL = [("a", wi, c) for wi in range(3) for c in range(KC)]
        FCL = [("g", i, 0, DM) for i in range(NCH)] \
            + [("u", i, 0, DM) for i in range(NCH)] \
            + [("d", i, pc, (768 if pc == 2 else 1024))
               for i in range(KC) for pc in range(3)]
        NA, NF = len(ACL), len(FCL)

        s_id = sem("s_id")
        s_cst = sem("s_cst")
        s_wld = [sem("s_wld0"), sem("s_wld1")]
        s_wcast = sem("s_wcast")
        s_q_ab = sem("s_q_ab")
        s_q_u = sem("s_q_u")
        s_q_s = sem("s_q_s")
        s_q_sg = sem("s_q_sg")
        s_q_q = sem("s_q_q")
        s_p0tr = sem("s_p0tr")
        s_p0ev = sem("s_p0ev")
        s_p0sp = [sem("s_p0sp0"), sem("s_p0sp1")]
        s_xld = [sem("s_xld0"), sem("s_xld1")]
        s_vst = sem("s_vst")
        s_ssq = sem("s_ssq")
        s_srs = sem("s_srs")
        s_lnd = sem("s_lnd")
        s_tr = sem("s_tr")
        s_trev = sem("s_trev")
        s_qmm = sem("s_qmm")
        s_qe = sem("s_qe")
        s_qev = sem("s_qev")
        s_qsp = sem("s_qsp")
        s_kmm = sem("s_kmm")
        s_ke = sem("s_ke")
        s_kev = sem("s_kev")
        s_vmm = sem("s_vmm")
        s_vev = sem("s_vev")
        s_kvr = sem("s_kvr")
        s_kvev = sem("s_kvev")

        def f_of(M):
            if M < 0:
                return []
            lo = NF * M // NMAC
            hi = NF * (M + 1) // NMAC
            return list(range(lo, hi))

        def f_sub(M, j):
            return [m for e, m in enumerate(f_of(M - 1)) if min(e, SPM - 1) == j]

        def fsrc(which, i, pc, F):
            src = {"g": wg_h, "u": wu_h, "d": wd_h}[which]
            return src[i * P:(i + 1) * P, pc * 1024:pc * 1024 + F]

        with ExitStack() as _pes:
            wst = _pes.enter_context(nc.sbuf_tensor("wst", [P, 2, DM], F32))
            p0u = _pes.enter_context(nc.sbuf_tensor("p0u", [P, 2, DM], F32))
            p0ab = _pes.enter_context(nc.sbuf_tensor("p0ab", [P, 2, DM], F32))
            p0v1 = _pes.enter_context(nc.sbuf_tensor("p0v1", [P, 2, DM], FP8))
            p0v2 = _pes.enter_context(nc.sbuf_tensor("p0v2", [P, 2, DM], FP8))
            p0r = _pes.enter_context(nc.sbuf_tensor("p0r", [P, 2, DM], BF16))
            p0B = _pes.enter_context(nc.sbuf_tensor("p0B", [P, 2, DM], BF16))
            p0c2 = _pes.enter_context(nc.sbuf_tensor("p0c2", [P, 2, DM], BF16))
            p0g = _pes.enter_context(nc.sbuf_tensor("p0g", [P, 2, DM], BF16))
            p0sg = _pes.enter_context(nc.sbuf_tensor("p0sg", [P, 2, DM], BF16))
            p0q = _pes.enter_context(nc.sbuf_tensor("p0q", [P, 6, DM], BF16))
            p0bm = _pes.enter_context(nc.sbuf_tensor("p0bm", [P, 2, DM // 16], F32))
            p0rr = _pes.enter_context(nc.sbuf_tensor("p0rr", [P, 2, DM // 16], F32))
            p0sb = _pes.enter_context(nc.sbuf_tensor("p0sb", [P, 2, DM // 16], BF16))
            p0st2 = _pes.enter_context(nc.sbuf_tensor("p0st2", [P, 2, DM], BF16))
            xbuf = _pes.enter_context(nc.sbuf_tensor("xbuf", [P, 2, SPM, DM], F32))
            stt1 = _pes.enter_context(nc.sbuf_tensor("stt1", [P, 24], F32))
            xn1 = _pes.enter_context(nc.sbuf_tensor("xn1", [P, 2, SPM, DM], BF16))
            xn1T = _pes.enter_context(nc.sbuf_tensor("xn1T", [P, KC, MACRO], BF16))
            qsb = _pes.enter_context(nc.sbuf_tensor("qsb", [P, KC, MACRO], BF16))
            fme = _pes.enter_context(nc.sbuf_tensor("fme", [P, 2, 512], BF16))
            fma = _pes.enter_context(nc.sbuf_tensor("fma", [P, 2, 512], BF16))
            ksb = _pes.enter_context(nc.sbuf_tensor("ksb", [P, DM], BF16))
            vsb = _pes.enter_context(nc.sbuf_tensor("vsb", [P, DM], BF16))
            kvp_ps = _pes.enter_context(nc.psum_tensor("kvp_ps", [P, KC, P], F32))
            ks_ps = _pes.enter_context(nc.psum_tensor("ks_ps", [P, KC], F32))
            q_ps0 = _pes.enter_context(nc.psum_tensor("q_ps0", [P, 512], F32))
            q_ps1 = _pes.enter_context(nc.psum_tensor("q_ps1", [P, 512], F32))
            k_ps = _pes.enter_context(nc.psum_tensor("k_ps", [P, DM], F32))
            tr_ps = _pes.enter_context(nc.psum_tensor("tr_ps", [P, KC * P], BF16))
            blk = _pes.enter_context(nc.Block())

            @blk.sync
            def _(sy):
                sy.dma_start(cst[:], cst_h[:]).then_inc(s_cst, 16)
                sy.dma_start(
                    xbuf[:, 0, :, :],
                    x_h[0:MACRO, :].rearrange("(j p) d -> p j d", p=P)) \
                    .then_inc(s_xld[0], 16)
                sy.dma_start(
                    xbuf[:, 1, :, :],
                    x_h[MACRO:2 * MACRO, :].rearrange("(j p) d -> p j d", p=P)) \
                    .then_inc(s_xld[1], 16)
                for li, (_, wi, c) in enumerate(ACL):
                    wh = [wqT_h, wkT_h, wvT_h][wi]
                    if li >= 2:
                        sy.wait_ge(s_wcast, li - 1)
                    sy.dma_start(wst[:, li % 2, :],
                                 wh[c * P:(c + 1) * P, :]) \
                        .then_inc(s_wld[li % 2], 16)
                for M in range(2, NMAC):
                    sy.wait_ge(s_lnd, M - 1)
                    sy.wait_ge(s_ssq, SPM * (M - 1))
                    sy.dma_start(
                        xbuf[:, M % 2, :, :],
                        x_h[M * MACRO:(M + 1) * MACRO, :]
                            .rearrange("(j p) d -> p j d", p=P)) \
                        .then_inc(s_xld[M % 2], 16)
                    for fi in f_of(M - 2):
                        which, i, pc, F = FCL[fi]
                        li = NA + fi
                        if fi >= 2:
                            sy.wait_ge(s_q_u, fi - 1)
                            sy.wait_ge(s_q_sg, fi - 1)
                        else:
                            sy.wait_ge(s_wcast, NA - 1 + fi)
                        sy.dma_start(wst[:, li % 2, 0:F], fsrc(which, i, pc, F)) \
                            .then_inc(s_wld[li % 2], 16)
                for fi in range(NF * (NMAC - 2) // NMAC, NF):
                    which, i, pc, F = FCL[fi]
                    li = NA + fi
                    sy.wait_ge(s_q_u, fi - 1)
                    sy.wait_ge(s_q_sg, fi - 1)
                    sy.dma_start(wst[:, li % 2, 0:F], fsrc(which, i, pc, F)) \
                        .then_inc(s_wld[li % 2], 16)

            @blk.gpsimd
            def _(g):
                g.wait_ge(s_id, 1)
                nc.gpsimd.affine_select(
                    out=ident[:], in_=ident[:], compare_op=Alu.not_equal,
                    fill=1.0, base=0, pattern=[[-1, P]], channel_multiplier=1,
                ).then_inc(s_id, 1)
                for M in range(NMAC):
                    g.wait_ge(s_qev, 8 * M + 8)
                    g.dma_start(q_s[M], qsb[:].rearrange("p c t -> p (c t)")) \
                        .then_inc(s_qsp, 16)
                    for fi in f_of(M - 1) if M >= 1 else []:
                        which, i, pc, F = FCL[fi]
                        npc = F // P
                        g.wait_ge(s_p0ev, fi + 1)
                        src = p0st2[:, fi % 2, 0:npc * P] \
                            .rearrange("p (c t) -> p c t", t=P)
                        if which == "g":
                            dst = bass.AP(qwg_s.tensor, i * P,
                                          [[DF, P], [P * DF, npc], [1, P]])
                        elif which == "u":
                            dst = bass.AP(qwu_s.tensor, i * P,
                                          [[DF, P], [P * DF, npc], [1, P]])
                        else:
                            dst = bass.AP(qwd_s.tensor,
                                          (8 * pc) * P * DM + i * P,
                                          [[DM, P], [P * DM, npc], [1, P]])
                        g.dma_start(dst, src).then_inc(s_p0sp[fi % 2], 16)
                for fi in f_of(NMAC - 1):
                    which, i, pc, F = FCL[fi]
                    npc = F // P
                    g.wait_ge(s_p0ev, fi + 1)
                    src = p0st2[:, fi % 2, 0:npc * P] \
                        .rearrange("p (c t) -> p c t", t=P)
                    if which == "g":
                        dst = bass.AP(qwg_s.tensor, i * P,
                                      [[DF, P], [P * DF, npc], [1, P]])
                    elif which == "u":
                        dst = bass.AP(qwu_s.tensor, i * P,
                                      [[DF, P], [P * DF, npc], [1, P]])
                    else:
                        dst = bass.AP(qwd_s.tensor, (8 * pc) * P * DM + i * P,
                                      [[DM, P], [P * DM, npc], [1, P]])
                    g.dma_start(dst, src).then_inc(s_p0sp[fi % 2], 16)
                g.wait_ge(s_qsp, 16 * NMAC)
                g.wait_ge(s_p0sp[0], 16 * ((NF + 1) // 2))
                g.wait_ge(s_p0sp[1], 16 * (NF // 2))

            def v_ffn_pair(v, fis):
                fa = fis[0]
                fb = fis[-1]
                sts, nbs = [], []
                for fi in fis:
                    which, i, pc, F = FCL[fi]
                    li = NA + fi
                    sts.append(wst[:, li % 2, 0:F])
                    nbs.append(F // 16)
                    v.wait_ge(s_wld[li % 2], 16 * (li // 2 + 1))
                if fa >= 1:
                    v.wait_ge(s_q_s, fa - 1 if fa >= 2 else fa)
                if fb >= 6:
                    v.wait_ge(s_p0tr, fb - 5)
                for fi, st, nb in zip(fis, sts, nbs):
                    nc.vector.tensor_reduce(
                        p0bm[:, fi % 2, 0:nb, None],
                        st.rearrange("p (b k) -> p b k", k=16),
                        AxX, Alu.max, apply_absolute_value=True)
                v.drain()
                for fi, nb in zip(fis, nbs):
                    nc.vector.reciprocal(p0rr[:, fi % 2, 0:nb],
                                         p0bm[:, fi % 2, 0:nb])
                    nc.vector.tensor_scalar(p0sb[:, fi % 2, 0:nb],
                                            p0bm[:, fi % 2, 0:nb],
                                            1.0 / 6.0, QCLAMP, Alu.mult, Alu.max)
                v.drain()
                v.wait_ge(s_q_ab, fb + 1)
                ins = None
                for fi, nb in zip(fis, nbs):
                    F = nb * 16
                    ins = nc.vector.tensor_tensor(
                        p0u[:, fi % 2, 0:F].rearrange("p (b k) -> p b k", k=16),
                        p0ab[:, fi % 2, 0:F].rearrange("p (b k) -> p b k", k=16),
                        p0rr[:, fi % 2, 0:nb, None].broadcast_to([P, nb, 16]),
                        Alu.mult).then_inc(s_q_u, 1)
                v.drain()
                for fi, nb in zip(fis, nbs):
                    F = nb * 16
                    nc.vector.tensor_scalar(p0c2[:, fi % 2, 0:F],
                                            p0u[:, fi % 2, 0:F],
                                            5.0, 2.0, Alu.is_gt, Alu.mult)
                v.wait_ge(s_q_s, fb + 1)
                for fi, nb in zip(fis, nbs):
                    F = nb * 16
                    nc.vector.scalar_tensor_tensor(
                        p0g[:, fi % 2, 0:F], p0B[:, fi % 2, 0:F], 2.0,
                        p0r[:, fi % 2, 0:F], Alu.min, Alu.subtract)
                v.drain()
                for fi, nb in zip(fis, nbs):
                    F = nb * 16
                    nc.vector.scalar_tensor_tensor(
                        p0g[:, fi % 2, 0:F], p0c2[:, fi % 2, 0:F], 2.0,
                        p0g[:, fi % 2, 0:F], Alu.add, Alu.add)
                v.drain()
                v.wait_ge(s_q_sg, fb + 1)
                for fi, nb in zip(fis, nbs):
                    F = nb * 16
                    nc.vector.tensor_mul(p0g[:, fi % 2, 0:F],
                                         p0g[:, fi % 2, 0:F],
                                         p0sg[:, fi % 2, 0:F])
                v.drain()
                for fi, nb in zip(fis, nbs):
                    F = nb * 16
                    nc.vector.tensor_tensor(
                        p0q[:, fi % 6, 0:F].rearrange("p (b k) -> p b k", k=16),
                        p0g[:, fi % 2, 0:F].rearrange("p (b k) -> p b k", k=16),
                        p0sb[:, fi % 2, 0:nb, None].broadcast_to([P, nb, 16]),
                        Alu.mult).then_inc(s_q_q, 1)

            def v_ffn_evac(v, ci):
                F = FCL[ci][3]
                v.wait_ge(s_p0tr, ci + 1)
                if ci >= 2:
                    v.wait_ge(s_p0sp[ci % 2], 16 * (ci // 2))
                nc.vector.tensor_copy(
                    p0st2[:, ci % 2, 0:F],
                    tr_ps[:, 0:F]).then_inc(s_p0ev, 1)

            @blk.vector
            def _(v):
                nc.vector.memzero(ident[:]).then_inc(s_id, 1)
                nc.vector.memset(ones1[:], 1.0)
                for li, (_, wi, c) in enumerate(ACL):
                    wsb = [wqT, wkT, wvT][wi]
                    v.wait_ge(s_wld[li % 2], 16 * (li // 2 + 1))
                    nc.vector.tensor_copy(wsb[:, c, :], wst[:, li % 2, :]) \
                        .then_inc(s_wcast, 1)
                mcol = stt1[:, 0:SPM]
                sscol = stt1[:, 4:4 + SPM]
                vcol = stt1[:, 8:8 + SPM]
                m2col = stt1[:, 12:12 + SPM]
                rcol = stt1[:, 16:16 + SPM]
                for it in range(NMAC + 1):
                    if it < NMAC:
                        xb = xbuf[:, it % 2, :, :]
                        v.wait_ge(s_xld[it % 2], 16 * (it // 2 + 1))
                        if it >= 1:
                            v.drain()
                        nc.vector.tensor_reduce(mcol[:, :, None], xb, AxX,
                                                Alu.add)
                        v.drain()
                        nc.vector.tensor_scalar(mcol, mcol, 1.0 / DM, None,
                                                Alu.mult)
                        v.drain()
                        nc.vector.tensor_mul(m2col, mcol, mcol)
                        v.drain()
                        v.wait_ge(s_ssq, SPM * (it + 1))
                        nc.vector.scalar_tensor_tensor(
                            vcol, sscol, 1.0 / DM, m2col,
                            Alu.mult, Alu.subtract).then_inc(s_vst, 1)
                        v.wait_ge(s_srs, it + 1)
                        if it >= 2:
                            v.wait_ge(s_tr, 4 * it - 4)
                        for j in range(SPM):
                            ins = nc.vector.tensor_scalar(
                                xn1[:, it % 2, j, :], xb[:, j, :],
                                mcol[:, j:j + 1], rcol[:, j:j + 1],
                                Alu.subtract, Alu.mult)
                        ins.then_inc(s_lnd, 1)
                    if it == 0:
                        continue
                    M = it - 1
                    sq = M // NMAC_S
                    m_in_s = M % NMAC_S
                    if M >= 1:
                        v.wait_ge(s_qsp, 16 * M)
                    for ic in range(KC):
                        v.wait_ge(s_qe, 2 * (8 * M + ic + 1))
                        nc.vector.scalar_tensor_tensor(
                            qsb[:, ic, :], fma[:, ic % 2, :], 1.0,
                            fme[:, ic % 2, :], Alu.add, Alu.min) \
                            .then_inc(s_qev, 1)
                    for j in range(SPM):
                        jj = SPM * M + j
                        if jj >= 1:
                            v.wait_ge(s_kvr, jj)
                        for h in range(2):
                            v.wait_ge(s_ke, 4 * jj + 2 * (h + 1))
                            nc.vector.scalar_tensor_tensor(
                                ksb[:, h * 512:(h + 1) * 512], fma[:, h, :],
                                1.0, fme[:, h, :], Alu.add, Alu.min) \
                                .then_inc(s_kev, 1)
                        for ci in f_sub(M, j):
                            v_ffn_evac(v, ci)
                    if m_in_s == NMAC_S - 1:
                        v.wait_ge(s_kvr, SPM * (M + 1))
                        nc.vector.memzero(kvb[:, sq, :, :])
                        nc.vector.memzero(KSb[:, sq, :, :])
                        v.drain()
                        nc.vector.tensor_copy(kvb[0:64, sq, :, 0:64],
                                              kvp_ps[0:64, :, 0:64])
                        nc.vector.tensor_copy(kvb[64:128, sq, :, 64:128],
                                              kvp_ps[64:128, :, 64:128])
                        ks_fs = n_seq * KC * 16
                        nc.vector.tensor_copy(
                            bass.AP(KSb, sq * KC * 16, [[ks_fs, 64], [18, KC]]),
                            ks_ps[0:64, :])
                        nc.vector.tensor_copy(
                            bass.AP(KSb, 64 * ks_fs + sq * KC * 16 + 1,
                                    [[ks_fs, 64], [18, KC]]),
                            ks_ps[64:128, :]).then_inc(s_kvev, 1)
                        v.drain()
                    g_m = f_of(M)
                    for e in range(0, len(g_m), 2):
                        v_ffn_pair(v, g_m[e:e + 2])
                for ci in f_of(NMAC - 1):
                    v_ffn_evac(v, ci)

            @blk.scalar
            def _(s):
                s.wait_ge(s_cst, 16)
                for it in range(NMAC + 1):
                    if it < NMAC:
                        xb = xbuf[:, it % 2, :, :]
                        s.wait_ge(s_xld[it % 2], 16 * (it // 2 + 1))
                        if it >= 1:
                            s.wait_ge(s_vst, it)
                            s.drain()
                        for j in range(SPM):
                            nc.scalar.activation(p0v1[:, 0, :], xb[:, j, :],
                                                 Act.Square,
                                                 accum_out=stt1[:, 4 + j:5 + j]) \
                                .then_inc(s_ssq, 1)
                            s.drain()
                        s.wait_ge(s_vst, it + 1)
                        nc.scalar.activation(stt1[:, 20:20 + SPM],
                                             stt1[:, 8:8 + SPM],
                                             Act.Ln, bias=LN_EPS)
                        s.drain()
                        nc.scalar.activation(stt1[:, 16:16 + SPM],
                                             stt1[:, 20:20 + SPM],
                                             Act.Exp, scale=-0.5) \
                            .then_inc(s_srs, 1)
                    if it == 0:
                        for j in range(SPM):
                            s.wait_ge(s_tr, j + 1)
                            nc.scalar.copy(
                                xn1T[:, :, j * P:(j + 1) * P],
                                tr_ps[:].rearrange("p (c t) -> p c t", t=P)) \
                                .then_inc(s_trev, 1)
                        continue
                    M = it - 1
                    for ic in range(KC):
                        ps = q_ps0 if ic % 2 == 0 else q_ps1
                        if ic >= 2:
                            s.wait_ge(s_qev, 8 * M + ic - 1)
                        s.wait_ge(s_qmm, 8 * M + ic + 1)
                        nc.scalar.activation(fme[:, ic % 2, :], ps[:],
                                             Act.Exp).then_inc(s_qe, 1)
                        nc.scalar.activation(fma[:, ic % 2, :], ps[:],
                                             Act.Relu).then_inc(s_qe, 1)
                    for j in range(SPM):
                        jj = SPM * M + j
                        s.wait_ge(s_kmm, jj + 1)
                        if jj == SPM * M:
                            s.wait_ge(s_qev, 8 * M + 8)
                        if jj >= 1:
                            s.wait_ge(s_kev, 2 * jj)
                        for h in range(2):
                            nc.scalar.activation(
                                fme[:, h, :], k_ps[:, h * 512:(h + 1) * 512],
                                Act.Exp).then_inc(s_ke, 1)
                            nc.scalar.activation(
                                fma[:, h, :], k_ps[:, h * 512:(h + 1) * 512],
                                Act.Relu).then_inc(s_ke, 1)
                        s.wait_ge(s_vmm, jj + 1)
                        if jj >= 1:
                            s.wait_ge(s_kvr, jj)
                        nc.scalar.copy(vsb[:, 0:512], q_ps0[:])
                        nc.scalar.copy(vsb[:, 512:1024], q_ps1[:]) \
                            .then_inc(s_vev, 1)
                    if it < NMAC:
                        for j in range(SPM):
                            s.wait_ge(s_tr, 4 * it + j + 1)
                            nc.scalar.copy(
                                xn1T[:, :, j * P:(j + 1) * P],
                                tr_ps[:].rearrange("p (c t) -> p c t", t=P)) \
                                .then_inc(s_trev, 1)
                    g_m = f_of(M)
                    for e in range(0, len(g_m), 2):
                        fis = g_m[e:e + 2]
                        fa, fb = fis[0], fis[-1]
                        if fa >= 2:
                            s.wait_ge(s_q_u, fa - 1)
                            s.wait_ge(s_q_q, fa - 1)
                        Fs = []
                        for fi in fis:
                            which, i, pc, F = FCL[fi]
                            li = NA + fi
                            Fs.append(F)
                            s.wait_ge(s_wld[li % 2], 16 * (li // 2 + 1))
                            nc.scalar.activation(
                                p0ab[:, fi % 2, 0:F], wst[:, li % 2, 0:F],
                                Act.Abs, scale=6.0).then_inc(s_q_ab, 1)
                            nc.scalar.activation(
                                p0sg[:, fi % 2, 0:F], wst[:, li % 2, 0:F],
                                Act.Sign).then_inc(s_q_sg, 1)
                        s.wait_ge(s_q_u, fb + 1)
                        for fi, F in zip(fis, Fs):
                            nc.scalar.activation(p0v1[:, fi % 2, 0:F],
                                                 p0u[:, fi % 2, 0:F],
                                                 Act.Copy, bias=4.0)
                            nc.scalar.activation(p0v2[:, fi % 2, 0:F],
                                                 p0u[:, fi % 2, 0:F],
                                                 Act.Copy, bias=7.0)
                        s.drain()
                        for fi, F in zip(fis, Fs):
                            nc.scalar.activation(p0r[:, fi % 2, 0:F],
                                                 p0v1[:, fi % 2, 0:F],
                                                 Act.Relu, bias=6.0, scale=-1.0)
                            nc.scalar.activation(p0B[:, fi % 2, 0:F],
                                                 p0v2[:, fi % 2, 0:F],
                                                 Act.Relu, bias=-9.0) \
                                .then_inc(s_q_s, 1)

            @blk.tensor
            def _(te):
                te.wait_ge(s_id, 2)
                te.wait_ge(s_wcast, 24)
                for M in range(NMAC):
                    te.wait_ge(s_lnd, M + 1)
                    if M >= 2:
                        te.wait_ge(s_p0ev, NF * (M - 1) // NMAC)
                    for j in range(SPM):
                        if 4 * M + j >= 1:
                            te.wait_ge(s_trev, 4 * M + j)
                        ins = None
                        for c in range(KC):
                            ins = nc.tensor.transpose(
                                tr_ps[:, c * P:(c + 1) * P],
                                xn1[:, M % 2, j, c * P:(c + 1) * P], ident[:])
                        ins.then_inc(s_tr, 1)
                    te.wait_ge(s_trev, 4 * M + 4)
                    for ic in range(KC):
                        ps = q_ps0 if ic % 2 == 0 else q_ps1
                        if ic >= 2:
                            te.wait_ge(s_qe, 2 * (8 * M + ic - 1))
                        elif M >= 1:
                            te.wait_ge(s_vev, SPM * M)
                        ins = None
                        for c in range(KC):
                            ins = nc.tensor.matmul(
                                ps[:], wqT[:, c, ic * P:(ic + 1) * P],
                                xn1T[:, c, :], start=(c == 0), stop=(c == KC - 1))
                        ins.then_inc(s_qmm, 1)
                    for j in range(SPM):
                        jj = SPM * M + j
                        if jj >= 1:
                            te.wait_ge(s_ke, 4 * jj)
                        ins = None
                        for c in range(KC):
                            lhs = xn1T[:, c, j * P:(j + 1) * P]
                            nc.tensor.matmul(
                                k_ps[:, 0:512], lhs, wkT[:, c, 0:512],
                                start=(c == 0), stop=(c == KC - 1))
                            ins = nc.tensor.matmul(
                                k_ps[:, 512:1024], lhs, wkT[:, c, 512:1024],
                                start=(c == 0), stop=(c == KC - 1))
                        ins.then_inc(s_kmm, 1)
                        if j == 0:
                            te.wait_ge(s_qe, 2 * (8 * M + 8))
                        else:
                            te.wait_ge(s_vev, jj)
                        ins = None
                        for c in range(KC):
                            lhs = xn1T[:, c, j * P:(j + 1) * P]
                            nc.tensor.matmul(
                                q_ps0[:], lhs, wvT[:, c, 0:512],
                                start=(c == 0), stop=(c == KC - 1))
                            ins = nc.tensor.matmul(
                                q_ps1[:], lhs, wvT[:, c, 512:1024],
                                start=(c == 0), stop=(c == KC - 1))
                        ins.then_inc(s_vmm, 1)
                        te.wait_ge(s_kev, 2 * jj + 2)
                        te.wait_ge(s_vev, jj + 1)
                        m_in_s = M % NMAC_S
                        sq_t = M // NMAC_S
                        if j == 0 and m_in_s == 0 and sq_t > 0:
                            te.wait_ge(s_kvev, sq_t)
                        first = (m_in_s == 0 and j == 0)
                        last = (m_in_s == NMAC_S - 1 and j == SPM - 1)
                        for pr in range(KC):
                            lhs = ksb[:, pr * P:(pr + 1) * P]
                            st = (first and (pr == 0 or pr == 4))
                            sp = (last and (pr == 3 or pr == 7))
                            nc.tensor.matmul(
                                kvp_ps[:, pr, :], lhs,
                                vsb[:, pr * P:(pr + 1) * P],
                                start=st, stop=sp)
                            ins = nc.tensor.matmul(
                                ks_ps[:, pr:pr + 1], lhs, ones1[:],
                                start=(first and pr == 0),
                                stop=(last and pr == KC - 1))
                        ins.then_inc(s_kvr, 1)
                        for ci in f_sub(M, j):
                            F = FCL[ci][3]
                            te.wait_ge(s_q_q, ci + 1)
                            if ci >= 1:
                                te.wait_ge(s_p0ev, ci)
                            ins = None
                            for k in range(F // P):
                                ins = nc.tensor.transpose(
                                    tr_ps[:, k * P:(k + 1) * P],
                                    p0q[:, ci % 6, k * P:(k + 1) * P], ident[:])
                            ins.then_inc(s_p0tr, 1)
                for ci in f_of(NMAC - 1):
                    F = FCL[ci][3]
                    te.wait_ge(s_q_q, ci + 1)
                    if ci >= 1:
                        te.wait_ge(s_p0ev, ci)
                    ins = None
                    for k in range(F // P):
                        ins = nc.tensor.transpose(
                            tr_ps[:, k * P:(k + 1) * P],
                            p0q[:, ci % 6, k * P:(k + 1) * P], ident[:])
                    ins.then_inc(s_p0tr, 1)

        attn3_es.close()
        s_qld = [sem("s_qld0"), sem("s_qld1")]
        s_qfr = sem("s_qfr")
        s_xld2 = [sem("s_xld20"), sem("s_xld21")]
        s_nd = sem("s_nd")
        s_dv = sem("s_dv")
        s_otr = sem("s_otr")
        s_oev = sem("s_oev")
        s_wo = sem("s_wo")
        s_x2 = sem("s_x2")
        s_x2sp = [sem("s_x2sp0"), sem("s_x2sp1")]
        s_vst2 = sem("s_vst2")
        s_ssq2 = sem("s_ssq2")
        s_srs2 = sem("s_srs2")
        s_2ab = sem("s_2ab")
        s_2xc = sem("s_2xc")
        s_2u = sem("s_2u")
        s_2s = sem("s_2s")
        s_2sg = sem("s_2sg")
        s_2g = sem("s_2g")
        s_2q = sem("s_2q")
        s_xtr = sem("s_xtr")
        s_xev = sem("s_xev")
        s_xqsp = sem("s_xqsp")
        s_wol = [sem("s_wol0"), sem("s_wol1")]
        s_woc = sem("s_woc")

        NBD = DM // 16

        with ExitStack() as _pes:
            woT = _pes.enter_context(nc.sbuf_tensor("woT_sb", [P, KC, DM], BF16))
            wos = _pes.enter_context(nc.sbuf_tensor("wos", [P, 2, DM], F32))
            qmac = _pes.enter_context(nc.sbuf_tensor("qmac", [P, 2, KC, MACRO], BF16))
            xbuf2 = _pes.enter_context(nc.sbuf_tensor("xbuf2", [P, 2, SPM, DM], F32))
            dnsb = _pes.enter_context(nc.sbuf_tensor("dnsb", [P, 16], F32))
            rcp = _pes.enter_context(nc.sbuf_tensor("rcp", [P, 16], F32))
            osb = _pes.enter_context(nc.sbuf_tensor("osb", [P, 2, DM], BF16))
            oT = _pes.enter_context(nc.sbuf_tensor("oT", [P, 2, KC, P], BF16))
            x2m = _pes.enter_context(nc.sbuf_tensor("x2m", [P, SPM, DM], F32))
            stt2 = _pes.enter_context(nc.sbuf_tensor("stt2", [P, 24], F32))
            junk2 = _pes.enter_context(nc.sbuf_tensor("junk2", [P, DM], BF16))
            xcm = _pes.enter_context(nc.sbuf_tensor("xcm", [P, SPM, DM], F32))
            q2u = _pes.enter_context(nc.sbuf_tensor("q2u", [P, 2, DM], F32))
            q2ab = _pes.enter_context(nc.sbuf_tensor("q2ab", [P, 2, DM], F32))
            q2v1 = _pes.enter_context(nc.sbuf_tensor("q2v1", [P, 2, DM], FP8))
            q2v2 = _pes.enter_context(nc.sbuf_tensor("q2v2", [P, 2, DM], FP8))
            q2r = _pes.enter_context(nc.sbuf_tensor("q2r", [P, 2, DM], BF16))
            q2B = _pes.enter_context(nc.sbuf_tensor("q2B", [P, 2, DM], BF16))
            q2c2 = _pes.enter_context(nc.sbuf_tensor("q2c2", [P, 2, DM], BF16))
            q2g = _pes.enter_context(nc.sbuf_tensor("q2g", [P, 2, DM], BF16))
            q2sg = _pes.enter_context(nc.sbuf_tensor("q2sg", [P, 2, DM], BF16))
            q2q = _pes.enter_context(nc.sbuf_tensor("q2q", [P, SPM, DM], BF16))
            q2bm = _pes.enter_context(nc.sbuf_tensor("q2bm", [P, 2, NBD], F32))
            q2sp = _pes.enter_context(nc.sbuf_tensor("q2sp", [P, 2, NBD], F32))
            q2rr = _pes.enter_context(nc.sbuf_tensor("q2rr", [P, 2, NBD], F32))
            q2sb = _pes.enter_context(nc.sbuf_tensor("q2sb", [P, 2, NBD], BF16))
            xqT = _pes.enter_context(nc.sbuf_tensor("xqT", [P, SPM, KC, P], BF16))
            num_ps = _pes.enter_context(nc.psum_tensor("num_ps", [P, KC, P], F32))
            den_ps = _pes.enter_context(nc.psum_tensor("den_ps", [P, 16], F32))
            trA_ps = _pes.enter_context(nc.psum_tensor("trA_ps", [P, KC * P], BF16))
            trB_ps = _pes.enter_context(nc.psum_tensor("trB_ps", [P, KC * P], BF16))
            at_ps = _pes.enter_context(nc.psum_tensor("at_ps", [P, DM], F32))
            blk = _pes.enter_context(nc.Block())

            @blk.sync
            def _(sy):
                for c in range(KC):
                    if c >= 2:
                        sy.wait_ge(s_woc, c - 1)
                    sy.dma_start(wos[:, c % 2, :], woT_h[c * P:(c + 1) * P, :]) \
                        .then_inc(s_wol[c % 2], 16)
                for M in range(NMAC):
                    if M >= 2:
                        sy.wait_ge(s_qfr, M - 1)
                        sy.wait_ge(s_x2, SPM * (M - 1))
                    sy.dma_start(qmac[:, M % 2, :, :],
                                 q_s[M].rearrange("p (c t) -> p c t", t=MACRO)) \
                        .then_inc(s_qld[M % 2], 16)
                    sy.dma_start(
                        xbuf2[:, M % 2, :, :],
                        x_h[M * MACRO:(M + 1) * MACRO, :]
                            .rearrange("(j p) d -> p j d", p=P)) \
                        .then_inc(s_xld2[M % 2], 16)

            @blk.gpsimd
            def _(g):
                for M in range(NMAC):
                    for j in range(SPM):
                        U = SPM * M + j
                        g.wait_ge(s_x2, U + 1)
                        g.dma_start(x2_s[U * SUB:(U + 1) * SUB, :],
                                    x2m[:, j, :]).then_inc(s_x2sp[U % 2], 16)
                    for j in range(SPM):
                        U = SPM * M + j
                        g.wait_ge(s_xev, SPM * M + j + 1)
                        g.dma_start(xq_s[U],
                                    xqT[:, j, :, :].rearrange("p c t -> p (c t)")) \
                            .then_inc(s_xqsp, 16)
                g.wait_ge(s_x2sp[0], 16 * ((NSUB + 1) // 2))
                g.wait_ge(s_x2sp[1], 16 * (NSUB // 2))
                g.wait_ge(s_xqsp, 16 * NSUB)

            @blk.tensor
            def _(te):
                def o_tail(Uo):
                    jo = Uo % SPM
                    te.wait_ge(s_dv, Uo + 1)
                    ins = None
                    for c in range(KC):
                        ins = nc.tensor.transpose(
                            trA_ps[:, c * P:(c + 1) * P],
                            osb[:, Uo % 2, c * P:(c + 1) * P], ident[:])
                        if jo == SPM - 1 and c == KC - 2:
                            ins.then_inc(s_qfr, 1)
                    ins.then_inc(s_otr, 1)
                    te.wait_ge(s_oev, Uo + 1)
                    if Uo == 0:
                        te.wait_ge(s_woc, KC)
                    if Uo >= 1:
                        te.wait_ge(s_x2, Uo)
                    ins = None
                    for c in range(KC):
                        lhs = oT[:, Uo % 2, c, :]
                        nc.tensor.matmul(at_ps[:, 0:512], lhs,
                                         woT[:, c, 0:512],
                                         start=(c == 0), stop=(c == KC - 1))
                        ins = nc.tensor.matmul(at_ps[:, 512:1024], lhs,
                                               woT[:, c, 512:1024],
                                               start=(c == 0),
                                               stop=(c == KC - 1))
                    ins.then_inc(s_wo, 1)

                for it in range(NMAC + 1):
                    if it >= 1:
                        Mp = it - 1
                        for j in range(SPM):
                            U = SPM * Mp + j
                            te.wait_ge(s_2q, U + 1)
                            if U >= 1:
                                te.wait_ge(s_xev, U)
                            ins = None
                            for c in range(KC):
                                ins = nc.tensor.transpose(
                                    trB_ps[:, c * P:(c + 1) * P],
                                    q2q[:, j, c * P:(c + 1) * P], ident[:])
                            ins.then_inc(s_xtr, 1)
                    if it < NMAC:
                        M = it
                        sq = M // NMAC_S
                        te.wait_ge(s_qld[M % 2], 16 * (M // 2 + 1))
                        for j in range(SPM):
                            U = SPM * M + j
                            if U >= 1:
                                te.wait_ge(s_dv, U)
                            ins = None
                            for c in range(KC):
                                lhs = qmac[:, M % 2, c, j * P:(j + 1) * P]
                                nc.tensor.matmul(
                                    num_ps[:, c, :], lhs, kvb[:, sq, c, :],
                                    start=True, stop=True)
                                ins = nc.tensor.matmul(
                                    den_ps[:], lhs, KSb[:, sq, c, :],
                                    start=(c == 0), stop=(c == KC - 1))
                            ins.then_inc(s_nd, 1)
                            o_tail(U)

            @blk.vector
            def _(v):
                for c in range(KC):
                    v.wait_ge(s_wol[c % 2], 16 * (c // 2 + 1))
                    nc.vector.tensor_copy(woT[:, c, :], wos[:, c % 2, :]) \
                        .then_inc(s_woc, 1)
                for M in range(NMAC):
                    mcol = stt2[:, 0:SPM]
                    sscol = stt2[:, 4:4 + SPM]
                    vcol = stt2[:, 8:8 + SPM]
                    m2col = stt2[:, 12:12 + SPM]
                    rcol = stt2[:, 16:16 + SPM]
                    v.wait_ge(s_xld2[M % 2], 16 * (M // 2 + 1))
                    for j in range(SPM):
                        U = SPM * M + j
                        v.wait_ge(s_nd, U + 1)
                        nc.vector.tensor_scalar(dnsb[:], den_ps[:], ATT_EPS,
                                                None, Alu.add)
                        v.drain()
                        nc.vector.reciprocal(rcp[:], dnsb[:])
                        v.drain()
                        if U >= 2:
                            v.wait_ge(s_otr, U - 1)
                        nc.vector.tensor_tensor(
                            osb[:, U % 2, :].rearrange("p (a b) -> p a b", b=64),
                            num_ps[:].rearrange("p c t -> p (c t)")
                                .rearrange("p (a b) -> p a b", b=64),
                            rcp[:, :, None].broadcast_to([P, 16, 64]),
                            Alu.mult).then_inc(s_dv, 1)
                        v.wait_ge(s_wo, U + 1)
                        if M >= 1 and j == 0:
                            v.wait_ge(s_x2sp[0], 16 * (SPM * M // 2))
                            v.wait_ge(s_x2sp[1], 16 * (SPM * M // 2))
                            v.wait_ge(s_ssq2, SPM * M)
                        nc.vector.tensor_add(x2m[:, j, :], at_ps[:],
                                             xbuf2[:, M % 2, j, :]) \
                            .then_inc(s_x2, 1)
                    v.drain()
                    nc.vector.tensor_reduce(mcol[:, :, None], x2m[:], AxX, Alu.add)
                    v.drain()
                    nc.vector.tensor_scalar(mcol, mcol, 1.0 / DM, None, Alu.mult)
                    v.drain()
                    nc.vector.tensor_mul(m2col, mcol, mcol)
                    v.drain()
                    v.wait_ge(s_ssq2, SPM * (M + 1))
                    nc.vector.scalar_tensor_tensor(
                        vcol, sscol, 1.0 / DM, m2col, Alu.mult, Alu.subtract) \
                        .then_inc(s_vst2, 1)
                    if M >= 1:
                        v.wait_ge(s_2sg, SPM * M)
                    for j in range(SPM):
                        nc.vector.tensor_scalar(xcm[:, j, :], x2m[:, j, :],
                                                mcol[:, j:j + 1], None,
                                                Alu.subtract).then_inc(s_2xc, 1)
                    v.drain()
                    v.wait_ge(s_srs2, M + 1)
                    for j0 in range(0, SPM, 2):
                        for j in (j0, j0 + 1):
                            sl = j % 2
                            nc.vector.tensor_reduce(
                                q2bm[:, sl, :, None],
                                xcm[:, j, :].rearrange("p (b k) -> p b k", k=16),
                                AxX, Alu.max, apply_absolute_value=True)
                        v.drain()
                        for j in (j0, j0 + 1):
                            sl = j % 2
                            nc.vector.tensor_scalar(
                                q2sp[:, sl, :], q2bm[:, sl, :],
                                rcol[:, j:j + 1], 1.0 / 6.0, Alu.mult, Alu.mult)
                        v.drain()
                        for j in (j0, j0 + 1):
                            sl = j % 2
                            nc.vector.tensor_scalar(q2sp[:, sl, :], q2sp[:, sl, :],
                                                    QCLAMP, None, Alu.max)
                        v.drain()
                        for j in (j0, j0 + 1):
                            sl = j % 2
                            nc.vector.reciprocal(q2rr[:, sl, :], q2sp[:, sl, :])
                            nc.vector.tensor_copy(q2sb[:, sl, :], q2sp[:, sl, :])
                        v.drain()
                        for j in (j0, j0 + 1):
                            sl = j % 2
                            nc.vector.tensor_scalar(q2rr[:, sl, :], q2rr[:, sl, :],
                                                    rcol[:, j:j + 1], None,
                                                    Alu.mult)
                        v.drain()
                        for j in (j0, j0 + 1):
                            U = SPM * M + j
                            sl = j % 2
                            v.wait_ge(s_2ab, U + 1)
                            if U >= 2:
                                v.wait_ge(s_2s, U - 1)
                            nc.vector.tensor_tensor(
                                q2u[:, sl, :].rearrange("p (b k) -> p b k", k=16),
                                q2ab[:, sl, :].rearrange("p (b k) -> p b k", k=16),
                                q2rr[:, sl, :, None].broadcast_to([P, NBD, 16]),
                                Alu.mult).then_inc(s_2u, 1)
                        v.drain()
                        for j in (j0, j0 + 1):
                            sl = j % 2
                            nc.vector.tensor_scalar(q2c2[:, sl, :], q2u[:, sl, :],
                                                    5.0, 2.0, Alu.is_gt, Alu.mult)
                        for j in (j0, j0 + 1):
                            U = SPM * M + j
                            sl = j % 2
                            v.wait_ge(s_2s, U + 1)
                            nc.vector.scalar_tensor_tensor(
                                q2g[:, sl, :], q2B[:, sl, :], 2.0, q2r[:, sl, :],
                                Alu.min, Alu.subtract)
                        v.drain()
                        for j in (j0, j0 + 1):
                            sl = j % 2
                            nc.vector.scalar_tensor_tensor(
                                q2g[:, sl, :], q2c2[:, sl, :], 2.0, q2g[:, sl, :],
                                Alu.add, Alu.add).then_inc(s_2g, 1)
                        v.drain()
                        for j in (j0, j0 + 1):
                            U = SPM * M + j
                            sl = j % 2
                            v.wait_ge(s_2sg, U + 1)
                            nc.vector.tensor_mul(q2g[:, sl, :], q2g[:, sl, :],
                                                 q2sg[:, sl, :])
                        v.drain()
                        for j in (j0, j0 + 1):
                            U = SPM * M + j
                            sl = j % 2
                            if M >= 1 and j == 0:
                                v.wait_ge(s_xtr, SPM * M)
                            nc.vector.tensor_tensor(
                                q2q[:, j, :].rearrange("p (b k) -> p b k", k=16),
                                q2g[:, sl, :].rearrange("p (b k) -> p b k", k=16),
                                q2sb[:, sl, :, None].broadcast_to([P, NBD, 16]),
                                Alu.mult).then_inc(s_2q, 1)

            @blk.scalar
            def _(s):
                for M in range(NMAC):
                    for j in range(SPM if M >= 1 else 0):
                        U = SPM * (M - 1) + j
                        s.wait_ge(s_xtr, U + 1)
                        if U >= 1:
                            s.wait_ge(s_xqsp, 16 * U)
                        nc.scalar.copy(
                            xqT[:, j, :, :].rearrange("p c t -> p (c t)"),
                            trB_ps[:]).then_inc(s_xev, 1)
                    for j in range(SPM):
                        U = SPM * M + j
                        s.wait_ge(s_otr, U + 1)
                        if U >= 2:
                            s.wait_ge(s_wo, U - 1)
                        nc.scalar.copy(oT[:, U % 2, :, :]
                                       .rearrange("p c t -> p (c t)"),
                                       trA_ps[:]).then_inc(s_oev, 1)
                        s.wait_ge(s_x2, U + 1)
                        nc.scalar.activation(junk2[:], x2m[:, j, :], Act.Square,
                                             accum_out=stt2[:, 4 + j:5 + j]) \
                            .then_inc(s_ssq2, 1)
                        s.drain()
                    s.wait_ge(s_vst2, M + 1)
                    nc.scalar.activation(stt2[:, 20:20 + SPM], stt2[:, 8:8 + SPM],
                                         Act.Ln, bias=LN_EPS)
                    s.drain()
                    nc.scalar.activation(stt2[:, 16:16 + SPM],
                                         stt2[:, 20:20 + SPM],
                                         Act.Exp, scale=-0.5).then_inc(s_srs2, 1)
                    for j in range(SPM):
                        U = SPM * M + j
                        sl = j % 2
                        s.wait_ge(s_2xc, U + 1)
                        if U >= 2:
                            s.wait_ge(s_2u, U - 1)
                        nc.scalar.activation(q2ab[:, sl, :], xcm[:, j, :],
                                             Act.Abs).then_inc(s_2ab, 1)
                        if U >= 2:
                            s.wait_ge(s_2q, U - 1)
                        nc.scalar.activation(q2sg[:, sl, :], xcm[:, j, :],
                                             Act.Sign).then_inc(s_2sg, 1)
                        s.wait_ge(s_2u, U + 1)
                        if U >= 2:
                            s.wait_ge(s_2g, U - 1)
                        nc.scalar.activation(q2v1[:, sl, :], q2u[:, sl, :],
                                             Act.Copy, bias=4.0)
                        nc.scalar.activation(q2v2[:, sl, :], q2u[:, sl, :],
                                             Act.Copy, bias=7.0)
                        s.drain()
                        nc.scalar.activation(q2r[:, sl, :], q2v1[:, sl, :],
                                             Act.Relu, bias=6.0, scale=-1.0)
                        nc.scalar.activation(q2B[:, sl, :], q2v2[:, sl, :],
                                             Act.Relu, bias=-9.0).then_inc(s_2s, 1)
                if True:
                    for j in range(SPM):
                        U = SPM * (NMAC - 1) + j
                        s.wait_ge(s_xtr, U + 1)
                        s.wait_ge(s_xqsp, 16 * U)
                        nc.scalar.copy(
                            xqT[:, j, :, :].rearrange("p c t -> p (c t)"),
                            trB_ps[:]).then_inc(s_xev, 1)

        wo_es.close()
        s_wres = sem("s_wres")
        s_x2l = [sem("s_x2l0"), sem("s_x2l1")]
        s_xql = [sem("s_xql0"), sem("s_xql1")]
        s_xqf = sem("s_xqf")
        s_gu = sem("s_gu")
        s_sil = sem("s_sil")
        s_h = sem("s_h")
        s_3ab = sem("s_3ab")
        s_3u = sem("s_3u")
        s_3s = sem("s_3s")
        s_3sg = sem("s_3sg")
        s_3g = sem("s_3g")
        s_3q = sem("s_3q")
        s_3tr = sem("s_3tr")
        s_3tev = sem("s_3tev")
        s_dnr = sem("s_dnr")
        s_wdres = sem("s_wdres")
        s_x3 = sem("s_x3")
        s_yst = [sem("s_yst0"), sem("s_yst1")]

        QCH = 704
        NQC = DF // QCH
        QNB = QCH // 16
        WAVES = [(0, 8), (8, 8), (16, 6)]
        CH_SLAB = [min(NSL - 1, (QCH * (c + 1) - 1) // 512) for c in range(NQC)]
        WV_CH = [min(NQC - 1, (P * (w0 + n) - 1) // QCH) for w0, n in WAVES]

        with ExitStack() as _pes:
            qwg = _pes.enter_context(nc.sbuf_tensor("qwg", [P, KC, DF], BF16))
            qwu = _pes.enter_context(nc.sbuf_tensor("qwu", [P, KC, DF], BF16))
            qwd = _pes.enter_context(nc.sbuf_tensor("qwd", [P, NCH, DM], BF16))
            x2b = _pes.enter_context(nc.sbuf_tensor("x2b", [P, 2, DM], F32))
            xqb = _pes.enter_context(nc.sbuf_tensor("xqb", [P, 2, KC, P], BF16))
            sgsb = _pes.enter_context(nc.sbuf_tensor("sgsb", [P, 512], F32))
            hsb = _pes.enter_context(nc.sbuf_tensor("hsb", [P, 2, DF], F32))
            q3u = _pes.enter_context(nc.sbuf_tensor("q3u", [P, 2, QCH], F32))
            q3ab = _pes.enter_context(nc.sbuf_tensor("q3ab", [P, 2, QCH], F32))
            q3v1 = _pes.enter_context(nc.sbuf_tensor("q3v1", [P, 2, QCH], FP8))
            q3v2 = _pes.enter_context(nc.sbuf_tensor("q3v2", [P, 2, QCH], FP8))
            q3r = _pes.enter_context(nc.sbuf_tensor("q3r", [P, 2, QCH], BF16))
            q3B = _pes.enter_context(nc.sbuf_tensor("q3B", [P, 2, QCH], BF16))
            q3c2 = _pes.enter_context(nc.sbuf_tensor("q3c2", [P, 2, QCH], BF16))
            q3g = _pes.enter_context(nc.sbuf_tensor("q3g", [P, 2, QCH], BF16))
            q3sg = _pes.enter_context(nc.sbuf_tensor("q3sg", [P, 2, QCH], BF16))
            q3q = _pes.enter_context(nc.sbuf_tensor("q3q", [P, DF], BF16))
            q3bm = _pes.enter_context(nc.sbuf_tensor("q3bm", [P, 2, QNB], F32))
            q3rsi = _pes.enter_context(nc.sbuf_tensor("q3rsi", [P, 2, QNB], F32))
            q3sb = _pes.enter_context(nc.sbuf_tensor("q3sb", [P, 2, QNB], BF16))
            qhT = _pes.enter_context(nc.sbuf_tensor("qhT", [P, NCH, P], BF16))
            gu_psA = _pes.enter_context(nc.psum_tensor("gu_psA", [P, DM], F32))
            gu_psB = _pes.enter_context(nc.psum_tensor("gu_psB", [P, DM], F32))
            dn_ps = _pes.enter_context(nc.psum_tensor("dn_ps", [P, DM], F32))
            qh_psA = _pes.enter_context(nc.psum_tensor("qh_psA", [P, 1024], BF16))
            qh_psB = _pes.enter_context(nc.psum_tensor("qh_psB", [P, 1024], BF16))
            blk = _pes.enter_context(nc.Block())

            @blk.sync
            def _(sy):
                sy.dma_start(qwg[:], qwg_s[:].rearrange("c p f -> p c f")) \
                    .then_inc(s_wres, 16)
                sy.dma_start(qwu[:], qwu_s[:].rearrange("c p f -> p c f")) \
                    .then_inc(s_wres, 16)
                sy.dma_start(qwd[:], qwd_s[:].rearrange("c p f -> p c f")) \
                    .then_inc(s_wdres, 16)
                for U in range(NSUB):
                    if U >= 2:
                        sy.wait_ge(s_xqf, U - 1)
                        sy.wait_ge(s_yst[U % 2], 16 * (U // 2))
                    sy.dma_start(x2b[:, U % 2, :], x2_s[U * SUB:(U + 1) * SUB, :]) \
                        .then_inc(s_x2l[U % 2], 16)
                    sy.dma_start(xqb[:, U % 2, :, :],
                                 xq_s[U].rearrange("p (c t) -> p c t", t=P)) \
                        .then_inc(s_xql[U % 2], 16)

            @blk.gpsimd
            def _(g):
                for U in range(NSUB):
                    g.wait_ge(s_x3, 2 * (U + 1))
                    g.dma_start(y_h[U * SUB:(U + 1) * SUB, :], x2b[:, U % 2, :]) \
                        .then_inc(s_yst[U % 2], 16)
                g.wait_ge(s_yst[0], 16 * ((NSUB + 1) // 2))
                g.wait_ge(s_yst[1], 16 * (NSUB // 2))

            @blk.tensor
            def _(te):
                te.wait_ge(s_wres, 32)
                for it in range(NSUB + 1):
                    if it < NSUB:
                        U = it
                        te.wait_ge(s_xql[U % 2], 16 * (U // 2 + 1))
                        for l, (n0, w) in enumerate(SLABS):
                            gu = gu_psA if l % 2 == 0 else gu_psB
                            li = NSL * U + l
                            if li >= 2:
                                te.wait_ge(s_h, li - 1)
                            ins = None
                            for c in range(KC):
                                lhs = xqb[:, U % 2, c, :]
                                nc.tensor.matmul(
                                    gu[:, 0:w], lhs, qwg[:, c, n0:n0 + w],
                                    start=(c == 0), stop=(c == KC - 1))
                                ins = nc.tensor.matmul(
                                    gu[:, 512:512 + w], lhs, qwu[:, c, n0:n0 + w],
                                    start=(c == 0), stop=(c == KC - 1))
                            ins.then_inc(s_gu, 1)
                    if it >= 1:
                        U = it - 1
                        for w_i, (p0, npc) in enumerate(WAVES):
                            qh_ps = qh_psA if w_i % 2 == 0 else qh_psB
                            te.wait_ge(s_3q, NQC * U + WV_CH[w_i] + 1)
                            if w_i == 2:
                                te.wait_ge(s_3tev, 3 * U + 1)
                            elif U >= 1:
                                te.wait_ge(s_3tev, 3 * U - w_i)
                            ins = None
                            for k in range(npc):
                                ins = nc.tensor.transpose(
                                    qh_ps[:, k * P:(k + 1) * P],
                                    q3q[:, (p0 + k) * P:(p0 + k + 1) * P],
                                    ident[:])
                                if w_i == 0 and k == 0:
                                    ins.then_inc(s_xqf, 1)
                            ins.then_inc(s_3tr, 1)
                        te.wait_ge(s_3tev, 3 * (U + 1))
                        if U == 0:
                            te.wait_ge(s_wdres, 16)
                        if U >= 1:
                            te.wait_ge(s_x3, 2 * U)
                        for hf in range(2):
                            ins = None
                            for n_c in range(NCH):
                                ins = nc.tensor.matmul(
                                    dn_ps[:, hf * 512:(hf + 1) * 512],
                                    qhT[:, n_c, :],
                                    qwd[:, n_c, hf * 512:(hf + 1) * 512],
                                    start=(n_c == 0), stop=(n_c == NCH - 1))
                            ins.then_inc(s_dnr, 1)

            @blk.scalar
            def _(s):
                for U in range(NSUB):
                    for l, (n0, w) in enumerate(SLABS):
                        gu = gu_psA if l % 2 == 0 else gu_psB
                        li = NSL * U + l
                        s.wait_ge(s_gu, li + 1)
                        if li >= 1:
                            s.wait_ge(s_h, li)
                        nc.scalar.activation(sgsb[:, 0:w], gu[:, 0:w], Act.Silu) \
                            .then_inc(s_sil, 1)
                    for w_i, (p0, npc) in enumerate(WAVES if U >= 1 else []):
                        qh_ps = qh_psA if w_i % 2 == 0 else qh_psB
                        s.wait_ge(s_3tr, 3 * (U - 1) + w_i + 1)
                        if U >= 2 and w_i == 0:
                            s.wait_ge(s_dnr, 2 * (U - 1))
                        nc.scalar.copy(
                            qhT[:, p0:p0 + npc, :].rearrange("p c t -> p (c t)"),
                            qh_ps[:, 0:npc * P]).then_inc(s_3tev, 1)

                    for c0 in range(0, NQC, 2):
                        for c in (c0, c0 + 1):
                            ci = NQC * U + c
                            sl = c % 2
                            hv = hsb[:, U % 2, QCH * c:QCH * (c + 1)]
                            if ci >= 2:
                                s.wait_ge(s_3u, ci - 1)
                            s.wait_ge(s_h, NSL * U + CH_SLAB[c] + 1)
                            nc.scalar.activation(q3ab[:, sl, :], hv, Act.Abs,
                                                 scale=6.0).then_inc(s_3ab, 1)
                            if ci >= 2:
                                s.wait_ge(s_3q, ci - 1)
                            nc.scalar.activation(q3sg[:, sl, :], hv, Act.Sign) \
                                .then_inc(s_3sg, 1)
                        for c in (c0, c0 + 1):
                            ci = NQC * U + c
                            sl = c % 2
                            s.wait_ge(s_3u, ci + 1)
                            if ci >= 2:
                                s.wait_ge(s_3g, ci - 1)
                            nc.scalar.activation(q3v1[:, sl, :], q3u[:, sl, :],
                                                 Act.Copy, bias=4.0)
                            nc.scalar.activation(q3v2[:, sl, :], q3u[:, sl, :],
                                                 Act.Copy, bias=7.0)
                        s.drain()
                        for c in (c0, c0 + 1):
                            sl = c % 2
                            nc.scalar.activation(q3r[:, sl, :], q3v1[:, sl, :],
                                                 Act.Relu, bias=6.0, scale=-1.0)
                            nc.scalar.activation(q3B[:, sl, :], q3v2[:, sl, :],
                                                 Act.Relu, bias=-9.0) \
                                .then_inc(s_3s, 1)
                for w_i, (p0, npc) in enumerate(WAVES):
                    qh_ps = qh_psA if w_i % 2 == 0 else qh_psB
                    s.wait_ge(s_3tr, 3 * (NSUB - 1) + w_i + 1)
                    if True and w_i == 0:
                        s.wait_ge(s_dnr, 2 * (NSUB - 1))
                    nc.scalar.copy(
                        qhT[:, p0:p0 + npc, :].rearrange("p c t -> p (c t)"),
                        qh_ps[:, 0:npc * P]).then_inc(s_3tev, 1)

            @blk.vector
            def _(v):
                for U in range(NSUB):
                    v.wait_ge(s_x2l[U % 2], 16 * (U // 2 + 1))
                    for l, (n0, w) in enumerate(SLABS):
                        gu = gu_psA if l % 2 == 0 else gu_psB
                        li = NSL * U + l
                        v.wait_ge(s_sil, li + 1)
                        if U >= 2 and l == 0:
                            v.wait_ge(s_3sg, NQC * (U - 1))
                        nc.vector.tensor_tensor(
                            hsb[:, U % 2, n0:n0 + w], sgsb[:, 0:w],
                            gu[:, 512:512 + w], Alu.mult).then_inc(s_h, 1)
                    for c0 in range(0, NQC, 2):
                        for c in (c0, c0 + 1):
                            sl = c % 2
                            hv = hsb[:, U % 2, QCH * c:QCH * (c + 1)]
                            nc.vector.tensor_reduce(
                                q3bm[:, sl, :, None],
                                hv.rearrange("p (b k) -> p b k", k=16),
                                AxX, Alu.max, apply_absolute_value=True)
                        v.drain()
                        for c in (c0, c0 + 1):
                            sl = c % 2
                            nc.vector.reciprocal(q3rsi[:, sl, :], q3bm[:, sl, :])
                            nc.vector.tensor_scalar(
                                q3sb[:, sl, :], q3bm[:, sl, :], 1.0 / 6.0,
                                QCLAMP, Alu.mult, Alu.max)
                        v.drain()
                        for c in (c0, c0 + 1):
                            ci = NQC * U + c
                            sl = c % 2
                            v.wait_ge(s_3ab, ci + 1)
                            if ci >= 2:
                                v.wait_ge(s_3s, ci - 1)
                            nc.vector.tensor_tensor(
                                q3u[:, sl, :].rearrange("p (b k) -> p b k", k=16),
                                q3ab[:, sl, :].rearrange("p (b k) -> p b k", k=16),
                                q3rsi[:, sl, :, None].broadcast_to([P, QNB, 16]),
                                Alu.mult).then_inc(s_3u, 1)
                        v.drain()
                        for c in (c0, c0 + 1):
                            sl = c % 2
                            nc.vector.tensor_scalar(q3c2[:, sl, :], q3u[:, sl, :],
                                                    5.0, 2.0, Alu.is_gt, Alu.mult)
                        for c in (c0, c0 + 1):
                            ci = NQC * U + c
                            sl = c % 2
                            v.wait_ge(s_3s, ci + 1)
                            nc.vector.scalar_tensor_tensor(
                                q3g[:, sl, :], q3B[:, sl, :], 2.0, q3r[:, sl, :],
                                Alu.min, Alu.subtract)
                        v.drain()
                        for c in (c0, c0 + 1):
                            sl = c % 2
                            nc.vector.scalar_tensor_tensor(
                                q3g[:, sl, :], q3c2[:, sl, :], 2.0, q3g[:, sl, :],
                                Alu.add, Alu.add).then_inc(s_3g, 1)
                        v.drain()
                        for c in (c0, c0 + 1):
                            ci = NQC * U + c
                            sl = c % 2
                            v.wait_ge(s_3sg, ci + 1)
                            nc.vector.tensor_mul(q3g[:, sl, :], q3g[:, sl, :],
                                                 q3sg[:, sl, :])
                        v.drain()
                        for c in (c0, c0 + 1):
                            ci = NQC * U + c
                            sl = c % 2
                            if U >= 1 and c == 0:
                                v.wait_ge(s_3tr, 3 * U)
                            nc.vector.tensor_tensor(
                                q3q[:, QCH * c:QCH * (c + 1)]
                                    .rearrange("p (b k) -> p b k", k=16),
                                q3g[:, sl, :].rearrange("p (b k) -> p b k", k=16),
                                q3sb[:, sl, :, None].broadcast_to([P, QNB, 16]),
                                Alu.mult).then_inc(s_3q, 1)
                        if U >= 1 and c0 == 0:
                            for hf in range(2):
                                v.wait_ge(s_dnr, 2 * (U - 1) + hf + 1)
                                nc.vector.tensor_add(
                                    x2b[:, (U - 1) % 2, hf * 512:(hf + 1) * 512],
                                    dn_ps[:, hf * 512:(hf + 1) * 512],
                                    x2b[:, (U - 1) % 2,
                                        hf * 512:(hf + 1) * 512]) \
                                    .then_inc(s_x3, 1)
                for hf in range(2):
                    v.wait_ge(s_dnr, 2 * (NSUB - 1) + hf + 1)
                    nc.vector.tensor_add(
                        x2b[:, (NSUB - 1) % 2, hf * 512:(hf + 1) * 512],
                        dn_ps[:, hf * 512:(hf + 1) * 512],
                        x2b[:, (NSUB - 1) % 2, hf * 512:(hf + 1) * 512]) \
                        .then_inc(s_x3, 1)

    return nc


def kernel(L0_attn_x, L0_attn_Wq, L0_attn_Wk, L0_attn_Wv, L0_attn_Wo,
           L0_ffn_w_gate, L0_ffn_w_up, L0_ffn_w_down):
    x = np.ascontiguousarray(np.asarray(L0_attn_x, dtype=np.float32))
    B, N, D = x.shape
    n_cores = 8
    per = B // n_cores
    wqT = np.ascontiguousarray(np.asarray(L0_attn_Wq, np.float32).T)
    wkT = np.ascontiguousarray(np.asarray(L0_attn_Wk, np.float32).T)
    wvT = np.ascontiguousarray(np.asarray(L0_attn_Wv, np.float32).T)
    woT = np.ascontiguousarray(np.asarray(L0_attn_Wo, np.float32).T)
    wg = np.ascontiguousarray(np.asarray(L0_ffn_w_gate, np.float32))
    wu = np.ascontiguousarray(np.asarray(L0_ffn_w_up, np.float32))
    wd = np.ascontiguousarray(np.asarray(L0_ffn_w_down, np.float32))

    nc = build_program(seq_len=N, n_seq=per)
    cstv = np.ascontiguousarray(np.broadcast_to(
        np.array([4.0, 7.0, 6.0, -1.0, -9.0, LN_EPS, -0.5, 1.0, 0.0],
                 np.float32), (128, 9)))
    in_maps = []
    for i in range(n_cores):
        in_maps.append({
            "x": np.ascontiguousarray(x[i * per:(i + 1) * per].reshape(per * N, D)),
            "wqT": wqT, "wkT": wkT, "wvT": wvT, "woT": woT,
            "wg": wg, "wu": wu, "wd": wd, "cst": cstv,
        })
    res = run_bass_kernel_spmd(nc, in_maps, core_ids=list(range(n_cores)))
    out = np.empty((B, N, D), np.float32)
    for i in range(n_cores):
        out[i * per:(i + 1) * per] = res.results[i]["y"].reshape(per, N, D)
    return out
